# version 20
# baseline (speedup 1.0000x reference)
import numpy as np
import concourse.bass as bass
import concourse.mybir as mybir
from concourse.bass_utils import run_bass_kernel_spmd

F32 = mybir.dt.float32
I32 = mybir.dt.int32
BF16 = mybir.dt.bfloat16
ALU = mybir.AluOpType
ACT = mybir.ActivationFunctionType
AX = mybir.AxisListType

E = 2048
E2 = 4096
HD = 128
TO = 256
HALO = 128
NSC = 4
S_CORE = NSC * TO
S_PAD = S_CORE + HALO


def _hadamard(n):
    H = np.array([[1.0]], dtype=np.float32)
    while H.shape[0] < n:
        H = np.block([[H, H], [H, -H]]).astype(np.float32)
    return H


class _Sem:
    __slots__ = ("name", "count", "h")

    def __init__(self, name):
        self.name = name
        self.count = 0
        self.h = None


class _Ctx:

    def __init__(self, nc):
        self.nc = nc
        self.streams = {k: [] for k in ("sync", "gpsimd", "tensor", "vector", "scalar")}
        self.sems = {k: _Sem(k) for k in ("LD", "GP", "T", "V", "S", "L2")}

    def emit(self, eng, fn, wait=(), inc=()):
        wait = [(s, v) for (s, v) in wait if v > 0]
        self.streams[eng].append((wait, fn, list(inc)))
        for s, n in inc:
            s.count += n

    def barrier(self):
        sems = list(self.sems.values())
        waits = [(x, x.count) for x in sems]
        for eng in self.streams:
            self.emit(eng, None, wait=list(waits))

    def replay(self, eng_name, e):
        for wait, fn, inc in self.streams[eng_name]:
            for s, v in wait:
                e.wait_ge(s.h, v)
            if fn is None:
                continue
            inst = fn(e)
            for s, n in inc:
                inst.then_inc(s.h, n)


def build_nc():
    nc = bass.Bass(trn_type="TRN2", target_bir_lowering=False, detect_race_conditions=False)

    x_in = nc.dram_tensor("x_pad", [S_PAD, E], F32, kind="ExternalInput").ap()
    wq1 = nc.dram_tensor("wq1", [E, HD], F32, kind="ExternalInput").ap()
    wq2 = nc.dram_tensor("wq2", [HD, E2], F32, kind="ExternalInput").ap()
    wk1 = nc.dram_tensor("wk1", [E, HD], F32, kind="ExternalInput").ap()
    wk2 = nc.dram_tensor("wk2", [HD, E2], F32, kind="ExternalInput").ap()
    wv = nc.dram_tensor("wv", [E, E2], F32, kind="ExternalInput").ap()
    wo = nc.dram_tensor("wo", [E2, E], F32, kind="ExternalInput").ap()
    nw_in = nc.dram_tensor("norm_w", [128, 32], F32, kind="ExternalInput").ap()
    wup_in = nc.dram_tensor("w_up", [E2, E], F32, kind="ExternalInput").ap()
    wdn_in = nc.dram_tensor("w_down", [E, E2], F32, kind="ExternalInput").ap()
    ident_in = nc.dram_tensor("ident", [128, 128], F32, kind="ExternalInput").ap()
    baup_in = nc.dram_tensor("ba_up", [128, 128], F32, kind="ExternalInput").ap()
    badn_in = nc.dram_tensor("ba_dn", [128, 128], F32, kind="ExternalInput").ap()
    h128_in = nc.dram_tensor("h128", [128, 128], F32, kind="ExternalInput").ap()
    out_hbm = nc.dram_tensor("out", [S_CORE, E], F32, kind="ExternalOutput").ap()

    zscr = nc.dram_tensor("zscr", [128, 16384], F32).ap()
    s_scr = nc.dram_tensor("s_scr", [128, 16384], F32).ap()
    wupT_scr = nc.dram_tensor("wupT_scr", [E, E2], BF16).ap()
    wdnT_scr = nc.dram_tensor("wdnT_scr", [E2, E], BF16).ap()

    from contextlib import ExitStack
    with ExitStack() as st:
        def sb(name, shape, dt=F32):
            return st.enter_context(nc.sbuf_tensor(name, shape, dt))

        wq1_sb = sb("wq1_sb", [128, E])
        wk1_sb = sb("wk1_sb", [128, E])
        nw_sb = sb("nw_sb", [128, 32])
        state = sb("state", [128, 32])
        ident = sb("ident_sb", [128, 128])
        baup = sb("baup_sb", [128, 128])
        badn = sb("badn_sb", [128, 128])
        h128 = sb("h128_sb", [128, 128])
        ones = sb("ones_sb", [128, 128])
        ident_bf = sb("identbf_sb", [128, 128], BF16)
        q1T = sb("q1T", [128, TO])
        lf1T = sb("lf1T", [128, 384])
        stats = sb("stats", [128, 32])
        aux = sb("aux", [128, 768])
        rmsacc = sb("rmsacc", [128, 2 * TO])
        RA = sb("RA", [128, 8192])
        RB = sb("RB", [128, 8192])
        RC = sb("RC", [128, 8192])
        SX = sb("SX", [128, 2 * E])
        XN = sb("XN", [128, 16 * 384])
        wvb = sb("wvb", [128, 2048])
        wqg = sb("wqg", [128, 1024])
        wkg = sb("wkg", [128, 1024])
        wob = sb("wob", [128, 4096])
        wub = sb("wub", [128, 2048], BF16)
        wdb = sb("wdb", [128, 4096], BF16)

        ps = [st.enter_context(nc.psum_tensor(f"ps{i}", [128, 512], F32)) for i in range(8)]

        C = _Ctx(nc)
        for s in C.sems.values():
            s.h = st.enter_context(nc.semaphore(s.name))
        LD, GP, T, V, S, L2 = (C.sems[k] for k in ("LD", "GP", "T", "V", "S", "L2"))

        xn_bf = XN[:].bitcast(BF16)

        C.emit("sync", lambda e: e.dma_start(out=ident[:], in_=ident_in), inc=[(LD, 16)])
        C.emit("sync", lambda e: e.dma_start(out=baup[:], in_=baup_in), inc=[(LD, 16)])
        C.emit("sync", lambda e: e.dma_start(out=badn[:], in_=badn_in), inc=[(LD, 16)])
        C.emit("sync", lambda e: e.dma_start(out=h128[:], in_=h128_in), inc=[(LD, 16)])
        C.emit("sync", lambda e: e.dma_start(out=nw_sb[:, 0:32], in_=nw_in), inc=[(LD, 16)])
        C.emit("sync", lambda e: e.dma_start(
            out=wq1_sb[:].rearrange("p (k m) -> p k m", k=16),
            in_=wq1.rearrange("(k p) m -> p k m", p=128)), inc=[(LD, 16)])
        C.emit("sync", lambda e: e.dma_start(
            out=wk1_sb[:].rearrange("p (k m) -> p k m", k=16),
            in_=wk1.rearrange("(k p) m -> p k m", p=128)), inc=[(LD, 16)])
        C.emit("vector", lambda e: e.memset(state[:], 0.0), inc=[(V, 1)])
        C.emit("vector", lambda e: e.memset(ones[:], 1.0), inc=[(V, 1)])

        wacc = aux
        ra_bf = RA[:].bitcast(BF16)
        C.emit("scalar", lambda e: e.copy(out=ident_bf[:], in_=ident[:]),
               wait=[(LD, 16)], inc=[(S, 1)])
        s_ib = S.count
        red_u, red_d, ld_u, ld_d = [], [], [], []
        for i in range(32):
            slot = (i % 2) * 4096
            src = wup_in[128 * i:128 * (i + 1), :]
            C.emit("sync", (lambda s_, src_: lambda e: e.dma_start(out=RA[:, s_:s_ + 2048], in_=src_))(slot, src),
                   wait=[(V, red_u[i - 2])] if i >= 2 else [], inc=[(LD, 16)])
            ld_u.append(LD.count)
            C.emit("vector", (lambda s_, i_: lambda e: e.tensor_reduce(
                out=wacc[:, i_:i_ + 1], in_=RA[:, s_:s_ + 2048], axis=AX.X, op=ALU.add,
                apply_absolute_value=True))(slot, i),
                wait=[(LD, ld_u[-1])], inc=[(V, 1)])
            red_u.append(V.count)
        C.emit("vector", lambda e: e.tensor_reduce(out=stats[:, 21:22], in_=wacc[:, 0:32],
                                                   axis=AX.X, op=ALU.add),
               wait=[(V, V.count)], inc=[(V, 1)])
        C.emit("tensor", lambda e: e.matmul(ps[0][0:1, 300:301], ones[:, 0:1], stats[:, 21:22],
                                            start=True, stop=True),
               wait=[(V, V.count)], inc=[(T, 1)])
        C.emit("vector", lambda e: e.tensor_copy(aux[0:1, 60:61], ps[0][0:1, 300:301]),
               wait=[(T, T.count)], inc=[(V, 1)])
        C.emit("tensor", lambda e: e.matmul(ps[1][:, 300:301], ones[0:1, 0:128], aux[0:1, 60:61],
                                            start=True, stop=True),
               wait=[(V, V.count)], inc=[(T, 1)])
        C.emit("vector", lambda e: e.tensor_scalar(out=stats[:, 8:9], in0=ps[1][:, 300:301],
                                                   scalar1=1.0 / (4096.0 * 2048.0), scalar2=1e-5,
                                                   op0=ALU.mult, op1=ALU.max),
               wait=[(T, T.count)], inc=[(V, 1)])
        C.emit("vector", lambda e: e.reciprocal(stats[:, 9:10], stats[:, 8:9]),
               wait=[(V, V.count)], inc=[(V, 1)])
        vscales_u = V.count
        for j in range(16):
            slot2 = (j % 2) * 4096
            src2 = wdn_in[128 * j:128 * (j + 1), :]
            C.emit("scalar", (lambda s_, src_: lambda e: e.dma_start(out=RB[:, s_:s_ + 4096], in_=src_))(slot2, src2),
                   wait=[(V, red_d[j - 2])] if j >= 2 else [], inc=[(L2, 16)])
            ld_d.append(L2.count)
            C.emit("vector", (lambda s_, j_: lambda e: e.tensor_reduce(
                out=wacc[:, 32 + j_:33 + j_], in_=RB[:, s_:s_ + 4096], axis=AX.X, op=ALU.add,
                apply_absolute_value=True))(slot2, j),
                wait=[(L2, ld_d[-1])], inc=[(V, 1)])
            red_d.append(V.count)
        vscales_d = 0

        def emit_wsd_chain():
            C.emit("vector", lambda e: e.tensor_reduce(out=stats[:, 22:23], in_=wacc[:, 32:48],
                                                       axis=AX.X, op=ALU.add),
                   wait=[(V, V.count)], inc=[(V, 1)])
            C.emit("tensor", lambda e: e.matmul(ps[0][0:1, 300:301], ones[:, 0:1], stats[:, 22:23],
                                                start=True, stop=True),
                   wait=[(V, V.count), (S, S.count)], inc=[(T, 1)])
            C.emit("vector", lambda e: e.tensor_copy(aux[0:1, 60:61], ps[0][0:1, 300:301]),
                   wait=[(T, T.count)], inc=[(V, 1)])
            C.emit("tensor", lambda e: e.matmul(ps[1][:, 300:301], ones[0:1, 0:128], aux[0:1, 60:61],
                                                start=True, stop=True),
                   wait=[(V, V.count), (S, S.count)], inc=[(T, 1)])
            C.emit("vector", lambda e: e.tensor_scalar(out=stats[:, 10:11], in0=ps[1][:, 300:301],
                                                       scalar1=1.0 / (2048.0 * 4096.0), scalar2=1e-5,
                                                       op0=ALU.mult, op1=ALU.max),
                   wait=[(T, T.count)], inc=[(V, 1)])
            C.emit("vector", lambda e: e.reciprocal(stats[:, 11:12], stats[:, 10:11]),
                   wait=[(V, V.count)], inc=[(V, 1)])


        blk_hist = []
        tp_done = []
        tp_grp = []
        stg = RC[:].bitcast(BF16)

        def w_quant_pass(win, nblk, ncols, rws_col, scrT, ktiles):
            for i in range(nblk):
                g = len(blk_hist)
                slot = (i % 2) * 4096
                src = win[128 * i:128 * (i + 1), :]
                if g >= 2:
                    h = blk_hist[g - 2]
                    dw = [(V, h["cmp"]), (T, h["tdone"]), (GP, h["gp"])]
                else:
                    dw = [(V, red_u[30 + g])]
                C.emit("sync", (lambda s_, src_, nc_: lambda e: e.dma_start(
                    out=RA[:, s_:s_ + nc_], in_=src_))(slot, src, ncols),
                    wait=dw, inc=[(LD, 16)])
                ldw = LD.count
                vsc = vscales_u if rws_col == 9 else vscales_d
                g1buf = RC[:, slot + 2048:slot + 2048 + ncols] if rws_col == 9 else RB[:, slot:slot + ncols]
                C.emit("vector", (lambda s_, nc_, g1_: lambda e: e.tensor_scalar(
                    out=g1_, in0=RA[:, s_:s_ + nc_], scalar1=stats[:, rws_col:rws_col + 1],
                    scalar2=0.5, op0=ALU.mult, op1=ALU.is_gt))(slot, ncols, g1buf),
                    wait=[(LD, ldw), (V, max(0, vsc))], inc=[(V, 1)])
                C.emit("vector", (lambda s_, nc_: lambda e: e.tensor_scalar(
                    out=RC[:, s_:s_ + nc_], in0=RA[:, s_:s_ + nc_], scalar1=stats[:, rws_col:rws_col + 1],
                    scalar2=-0.5, op0=ALU.mult, op1=ALU.is_lt))(slot, ncols), inc=[(V, 1)])
                C.emit("vector", (lambda s_, nc_, g1_: lambda e: e.tensor_tensor(
                    out=ra_bf[:, 2 * s_:2 * s_ + nc_], in0=g1_, in1=RC[:, s_:s_ + nc_],
                    op=ALU.subtract))(slot, ncols, g1buf), inc=[(V, 1)])
                vt = V.count
                stg_off = (i % 2) * 8192
                for k in range(ktiles):
                    pk = len(tp_done)
                    grp = pk // 4
                    pb = ps[grp % 8]
                    off = 64 * (pk % 4)
                    C.emit("tensor", (lambda s_, k_, pb_, o_: lambda e: e.transpose(
                        pb_[:, o_:o_ + 64].bitcast(BF16),
                        ra_bf[:, 2 * s_ + 128 * k_:2 * s_ + 128 * (k_ + 1)],
                        ident_bf[:]))(slot, k, pb, off),
                        wait=([(V, vt)] if k == 0 else []) +
                             ([(S, tp_grp[grp - 8])] if (pk % 4 == 0 and grp >= 8) else
                              ([(S, s_ib)] if pk == 0 else [])),
                        inc=[(T, 1)])
                    tp_done.append(None)
                    if pk % 4 == 3:
                        tc = T.count
                        C.emit("scalar", (lambda so_, k_, pb_: lambda e: e.copy(
                            out=stg[:, so_ + 128 * (k_ - 3):so_ + 128 * (k_ + 1)],
                            in_=pb_[:, 0:256].bitcast(BF16)))(stg_off, k, pb),
                            wait=[(T, tc)] + ([(GP, blk_hist[g - 2]["gp"])] if (k == 3 and g >= 2) else []),
                            inc=[(S, 1)])
                        tp_grp.append(S.count)
                sc = S.count
                dst = scrT.rearrange("(k p) o -> p k o", p=128)[:, :, 128 * i:128 * (i + 1)]
                C.emit("gpsimd", (lambda so_, dst_, k_: lambda e: e.dma_start(
                    out=dst_, in_=stg[:, so_:so_ + 128 * k_].rearrange("p (k o) -> p k o", k=k_)))(stg_off, dst, ktiles),
                    wait=[(S, sc)], inc=[(GP, 16)])
                blk_hist.append({"cmp": vt, "tdone": T.count, "gp": GP.count, "sdone": sc})

        w_quant_pass(wup_in, 32, 2048, 9, wupT_scr, 16)
        emit_wsd_chain()
        vscales_d = V.count
        w_quant_pass(wdn_in, 16, 4096, 11, wdnT_scr, 32)
        gp_wdone = GP.count

        C.barrier()
        def emit_attn_half(sc):
            ts = 384 if sc == 0 else 256
            ofs = 0 if sc == 0 else 384 + 256 * (sc - 1)
            nt = ts // 128
            oofs = 128 if sc == 0 else 0

            C.barrier()
            for i in range(nt):
                owned_i = i - (1 if sc == 0 else 0)
                xst = 4096
                C.emit("sync", (lambda r0: lambda e: e.dma_start(
                    out=RB[:, xst:xst + 2048], in_=x_in[r0:r0 + 128, :]))(ofs + 128 * i),
                    wait=[(V, V.count), (T, T.count), (GP, GP.count)], inc=[(LD, 16)])
                ldx = LD.count
                C.emit("vector", lambda e: e.tensor_reduce(
                    out=stats[:, 5:6], in_=RB[:, xst:xst + 2048], axis=AX.X, op=ALU.add),
                    wait=[(LD, ldx)], inc=[(V, 1)])
                C.emit("scalar", lambda e: e.activation(
                    out=RC[:, 0:2048], in_=RB[:, xst:xst + 2048], func=ACT.Square,
                    accum_out=stats[:, 6:7]),
                    wait=[(LD, ldx)], inc=[(S, 1)])
                C.emit("vector", lambda e: e.tensor_scalar(
                    out=stats[:, 0:1], in0=stats[:, 5:6], scalar1=1.0 / 2048.0, scalar2=None,
                    op0=ALU.mult), wait=[(S, S.count), (V, V.count)], inc=[(V, 1)])
                C.emit("vector", lambda e: e.tensor_scalar(
                    out=stats[:, 1:2], in0=stats[:, 6:7], scalar1=1.0 / 2048.0, scalar2=1e-5,
                    op0=ALU.mult, op1=ALU.add), inc=[(V, 1)])
                C.emit("scalar", lambda e: e.square(stats[:, 2:3], stats[:, 0:1]),
                       wait=[(V, V.count)], inc=[(S, 1)])
                C.emit("vector", lambda e: e.tensor_tensor(
                    out=stats[:, 7:8], in0=stats[:, 1:2], in1=stats[:, 2:3], op=ALU.subtract),
                    wait=[(S, S.count), (V, V.count)], inc=[(V, 1)])
                C.emit("scalar", lambda e: e.sqrt(stats[:, 3:4], stats[:, 7:8]),
                       wait=[(V, V.count)], inc=[(S, 1)])
                C.emit("vector", lambda e: e.reciprocal(stats[:, 4:5], stats[:, 3:4]),
                       wait=[(S, S.count)], inc=[(V, 1)])
                C.emit("vector", lambda e: e.tensor_scalar(
                    out=RC[:, 4096:6144], in0=RB[:, xst:xst + 2048], scalar1=stats[:, 0:1],
                    scalar2=stats[:, 4:5], op0=ALU.subtract, op1=ALU.mult),
                    wait=[(T, T.count), (V, V.count)], inc=[(V, 1)])
                vxn = V.count
                for k in range(16):
                    pb = ps[k % 2]
                    C.emit("tensor", (lambda k_, pb_: lambda e: e.transpose(
                        pb_[:, 0:128], RC[:, 4096 + 128 * k_:4096 + 128 * (k_ + 1)], ident[:]))(k, pb),
                        wait=[(V, vxn), (S, S.count)], inc=[(T, 1)])
                    tc = T.count
                    C.emit("scalar", (lambda k_, i_, pb_: lambda e: e.copy(
                        out=XN[:, 384 * k_ + 128 * i_:384 * k_ + 128 * (i_ + 1)], in_=pb_[:, 0:128]))(k, i, pb),
                        wait=[(T, tc)], inc=[(S, 1)])
            s_xn = S.count
            for (w1sb, dstT, n0, nlen) in ((wq1_sb, q1T, oofs, TO), (wk1_sb, lf1T, 0, ts)):
                for k in range(16):
                    C.emit("tensor", (lambda k_, w_, n0_, nl_: lambda e: e.matmul(
                        ps[2][:, 0:nl_], w_[:, 128 * k_:128 * (k_ + 1)],
                        XN[:, 384 * k_ + n0_:384 * k_ + n0_ + nl_],
                        start=(k_ == 0), stop=(k_ == 15)))(k, w1sb, n0, nlen),
                        wait=([(S, s_xn), (V, V.count)] if k == 0 else []),
                        inc=([(T, 1)] if k == 15 else []))
                tq = T.count
                C.emit("vector", (lambda d_, nl_: lambda e: e.tensor_copy(d_[:, 0:nl_], ps[2][:, 0:nl_]))(dstT, nlen),
                       wait=[(T, tq)], inc=[(V, 1)])
            v_q1 = V.count
            for g in range(4):
                gw = [(V, v_q1)] if g == 0 else []
                C.emit("sync", (lambda g_: lambda e: e.dma_start(
                    out=wqg[:], in_=wq2[:, 1024 * g_:1024 * (g_ + 1)]))(g),
                    wait=gw + [(T, T.count)], inc=[(LD, 16)])
                C.emit("sync", (lambda g_: lambda e: e.dma_start(
                    out=wkg[:], in_=wk2[:, 1024 * g_:1024 * (g_ + 1)]))(g), inc=[(LD, 16)])
                ld_qk = LD.count
                for m in range(8):
                    C.emit("tensor", (lambda m_: lambda e: e.matmul(
                        ps[m_][:, 0:ts], wkg[:, 128 * m_:128 * (m_ + 1)], lf1T[:, 0:ts],
                        start=True, stop=True))(m),
                        wait=([(LD, ld_qk), (S, S.count), (V, V.count)] if m == 0 else []),
                        inc=[(T, 1)])
                    tlf = T.count
                    C.emit("scalar", (lambda m_, ts_: lambda e: e.activation(
                        out=RC[:, 4096 + ts_ * m_:4096 + ts_ * (m_ + 1)], in_=ps[m_][:, 0:ts_],
                        func=ACT.Sigmoid))(m, ts),
                        wait=[(T, tlf)] + ([(V, V.count)] if m == 0 else []), inc=[(S, 1)])
                    C.emit("scalar", (lambda m_, ts_: lambda e: e.activation(
                        out=RA[:, ts_ * m_:ts_ * (m_ + 1)], in_=RC[:, 4096 + ts_ * m_:4096 + ts_ * (m_ + 1)],
                        func=ACT.Sigmoid))(m, ts),
                        wait=[(GP, GP.count)] if (g == 0 and m == 0) else [], inc=[(S, 1)])
                s_a = S.count
                C.emit("vector", lambda e: e.tensor_scalar(
                    out=RC[:, 4096:4096 + 8 * ts], in0=RA[:, 0:8 * ts], scalar1=-1.0, scalar2=1.0,
                    op0=ALU.mult, op1=ALU.add), wait=[(S, s_a)], inc=[(V, 1)])
                v_k = V.count
                tk = []
                for k in range(16):
                    slot = (k % 2) * 1024
                    C.emit("sync", (lambda g_, k_, s_: lambda e: e.dma_start(
                        out=wvb[:, s_:s_ + 1024], in_=wv[128 * k_:128 * (k_ + 1), 1024 * g_:1024 * (g_ + 1)]))(g, k, slot),
                        wait=[(T, tk[k - 2])] if k >= 2 else [(S, s_a)], inc=[(LD, 16)])
                    ldv = LD.count
                    for m in range(8):
                        C.emit("tensor", (lambda k_, m_, s_: lambda e: e.matmul(
                            ps[m_][:, 0:ts], wvb[:, s_ + 128 * m_:s_ + 128 * (m_ + 1)],
                            XN[:, 384 * k_:384 * k_ + ts],
                            start=(k_ == 0), stop=(k_ == 15)))(k, m, slot),
                            wait=([(LD, ldv)] if m == 0 else []) + ([(V, v_k)] if k == 0 and m == 0 else []),
                            inc=([(T, 1)] if m == 7 else []))
                    tk.append(T.count)
                t_v = T.count
                for m in range(8):
                    C.emit("vector", (lambda m_: lambda e: e.tensor_tensor(
                        out=RB[:, ts * m_:ts * (m_ + 1)], in0=RC[:, 4096 + ts * m_:4096 + ts * (m_ + 1)],
                        in1=ps[m_][:, 0:ts], op=ALU.mult))(m),
                        wait=[(T, t_v)] if m == 0 else [], inc=[(V, 1)])
                    C.emit("vector", (lambda g_, m_: lambda e: e.tensor_tensor_scan(
                        out=RC[:, ts * m_:ts * (m_ + 1)], data0=RA[:, ts * m_:ts * (m_ + 1)],
                        data1=RB[:, ts * m_:ts * (m_ + 1)], initial=state[:, 8 * g_ + m_:8 * g_ + m_ + 1],
                        op0=ALU.mult, op1=ALU.add))(g, m), inc=[(V, 1)])
                    C.emit("vector", (lambda g_, m_: lambda e: e.tensor_copy(
                        state[:, 8 * g_ + m_:8 * g_ + m_ + 1], RC[:, ts * m_ + ts - 1:ts * m_ + ts]))(g, m),
                        wait=[(V, V.count)], inc=[(V, 1)])
                v_scan = V.count
                for m in range(8):
                    C.emit("tensor", (lambda m_: lambda e: e.matmul(
                        ps[m_][:, 0:TO], wqg[:, 128 * m_:128 * (m_ + 1)], q1T[:, 0:TO],
                        start=True, stop=True))(m),
                        wait=([(V, v_scan)] if m == 0 else []), inc=[(T, 1)])
                    tqm = T.count
                    C.emit("scalar", (lambda m_: lambda e: e.activation(
                        out=RA[:, TO * m_:TO * (m_ + 1)], in_=ps[m_][:, 0:TO], func=ACT.Silu))(m),
                        wait=[(T, tqm)], inc=[(S, 1)])
                s_q = S.count
                for m in range(8):
                    C.emit("vector", (lambda m_: lambda e: e.tensor_tensor(
                        out=RB[:, TO * m_:TO * (m_ + 1)], in0=RA[:, TO * m_:TO * (m_ + 1)],
                        in1=RC[:, ts * m_ + oofs:ts * m_ + oofs + TO], op=ALU.mult))(m),
                        wait=[(S, s_q)] if m == 0 else [], inc=[(V, 1)])
                v_out = V.count
                for m in range(8):
                    C.emit("scalar", (lambda m_: lambda e: e.square(
                        RC[:, 4096 + TO * m_:4096 + TO * (m_ + 1)], RB[:, TO * m_:TO * (m_ + 1)]))(m),
                        wait=[(V, v_out)] if m == 0 else [], inc=[(S, 1)])
                s_sq = S.count
                SL = 6144
                for j in range(4):
                    C.emit("vector", (lambda j_: lambda e: e.tensor_tensor(
                        out=RC[:, SL + TO * j_:SL + TO * (j_ + 1)],
                        in0=RC[:, 4096 + TO * (2 * j_):4096 + TO * (2 * j_ + 1)],
                        in1=RC[:, 4096 + TO * (2 * j_ + 1):4096 + TO * (2 * j_ + 2)], op=ALU.add))(j),
                        wait=[(S, s_sq)] if j == 0 else [], inc=[(V, 1)])
                for j in range(2):
                    C.emit("vector", (lambda j_: lambda e: e.tensor_tensor(
                        out=RC[:, 4096 + TO * j_:4096 + TO * (j_ + 1)],
                        in0=RC[:, SL + TO * (2 * j_):SL + TO * (2 * j_ + 1)],
                        in1=RC[:, SL + TO * (2 * j_ + 1):SL + TO * (2 * j_ + 2)], op=ALU.add))(j),
                        inc=[(V, 1)])
                C.emit("vector", lambda e: e.tensor_tensor(
                    out=RC[:, SL:SL + TO], in0=RC[:, 4096:4096 + TO],
                    in1=RC[:, 4096 + TO:4096 + 2 * TO], op=ALU.add), inc=[(V, 1)])
                if g == 0:
                    C.emit("vector", lambda e: e.tensor_copy(rmsacc[:, 0:TO], RC[:, SL:SL + TO]), inc=[(V, 1)])
                else:
                    C.emit("vector", (lambda g_: lambda e: e.tensor_tensor(
                        out=rmsacc[:, TO * (g_ % 2):TO * (g_ % 2) + TO],
                        in0=rmsacc[:, TO * ((g_ - 1) % 2):TO * ((g_ - 1) % 2) + TO],
                        in1=RC[:, SL:SL + TO], op=ALU.add))(g), inc=[(V, 1)])
                for m in range(8):
                    mg = 8 * g + m
                    half, loc = (RA, mg) if mg < 16 else (RB, mg - 16)
                    C.emit("vector", (lambda m_, h_, l_, mg_: lambda e: e.tensor_scalar(
                        out=h_[:, 3072 + TO * l_:3072 + TO * (l_ + 1)], in0=RB[:, TO * m_:TO * (m_ + 1)],
                        scalar1=nw_sb[:, mg_:mg_ + 1], scalar2=None, op0=ALU.mult))(m, half, loc, mg),
                        inc=[(V, 1)])
            v_att = V.count
            C.emit("tensor", lambda e: e.matmul(ps[0][0:1, 0:TO], ones[:, 0:1], rmsacc[:, TO:2 * TO],
                                                start=True, stop=True),
                   wait=[(V, v_att)], inc=[(T, 1)])
            C.emit("vector", lambda e: e.tensor_scalar(
                out=aux[0:1, 0:TO], in0=ps[0][0:1, 0:TO], scalar1=1.0 / 4096.0, scalar2=1e-6,
                op0=ALU.mult, op1=ALU.add), wait=[(T, T.count)], inc=[(V, 1)])
            C.emit("scalar", lambda e: e.sqrt(aux[0:1, 256:256 + TO], aux[0:1, 0:TO]),
                   wait=[(V, V.count)], inc=[(S, 1)])
            C.emit("vector", lambda e: e.reciprocal(aux[0:1, 512:512 + TO], aux[0:1, 256:256 + TO]),
                   wait=[(S, S.count), (V, V.count)], inc=[(V, 1)])
            v_rr = V.count
            for i in range(2):
                C.emit("tensor", (lambda i_: lambda e: e.transpose(
                    ps[1 + i_][:, 0:1], aux[0:1, 512 + 128 * i_:512 + 128 * (i_ + 1)], ident[0:1, 0:1]))(i),
                    wait=[(V, v_rr if i == 0 else V.count)], inc=[(T, 1)])
                C.emit("vector", (lambda i_: lambda e: e.tensor_copy(
                    stats[:, 28 + i_:29 + i_], ps[1 + i_][:, 0:1]))(i),
                    wait=[(T, T.count)], inc=[(V, 1)])
            v_rrms = V.count
            tk = []
            for k in range(32):
                slot = (k % 2) * 2048
                half, loc = (RA, k) if k < 16 else (RB, k - 16)
                C.emit("sync", (lambda k_, s_: lambda e: e.dma_start(
                    out=wob[:, s_:s_ + 2048], in_=wo[128 * k_:128 * (k_ + 1), :]))(k, slot),
                    wait=[(T, tk[k - 2])] if k >= 2 else [(V, v_rrms)], inc=[(LD, 16)])
                ldo = LD.count
                for mc in range(2):
                    for n in range(4):
                        C.emit("tensor", (lambda k_, mc_, n_, s_, h_, l_: lambda e: e.matmul(
                            ps[mc_ * 4 + n_][:, 0:512],
                            h_[:, 3072 + TO * l_ + 128 * mc_:3072 + TO * l_ + 128 * (mc_ + 1)],
                            wob[:, s_ + 512 * n_:s_ + 512 * (n_ + 1)],
                            start=(k_ == 0), stop=(k_ == 31)))(k, mc, n, slot, half, loc),
                            wait=([(LD, ldo)] if (mc == 0 and n == 0) else []),
                            inc=([(T, 1)] if (mc == 1 and n == 3) else []))
                tk.append(T.count)
            t_wo = T.count
            for mc in range(2):
                for n in range(4):
                    C.emit("vector", (lambda mc_, n_: lambda e: e.tensor_scalar(
                        out=SX[:, 2048 * mc_ + 512 * n_:2048 * mc_ + 512 * (n_ + 1)],
                        in0=ps[mc_ * 4 + n_][:, 0:512], scalar1=stats[:, 28 + mc_:29 + mc_],
                        scalar2=None, op0=ALU.mult))(mc, n),
                        wait=[(T, t_wo), (V, V.count)] if (mc == 0 and n == 0) else [], inc=[(V, 1)])
            v_s = V.count
            C.emit("gpsimd", (lambda sc_: lambda e: e.dma_start(
                out=s_scr[:, 4096 * sc_:4096 * (sc_ + 1)], in_=SX[:]))(sc),
                wait=[(V, v_s), (GP, GP.count)], inc=[(GP, 16)])
            for b in range(16):
                pb = ps[b % 2]
                C.emit("tensor", (lambda b_, pb_: lambda e: e.matmul(
                    pb_[:, 0:TO], baup[:], XN[:, 384 * b_ + oofs:384 * b_ + oofs + TO],
                    start=True, stop=True))(b, pb),
                    wait=[(V, v_s)] if b == 0 else [(V, V.count)], inc=[(T, 1)])
                tb = T.count
                C.emit("vector", (lambda b_, pb_: lambda e: e.tensor_copy(
                    RA[:, TO * b_:TO * (b_ + 1)], pb_[:, 0:TO]))(b, pb),
                    wait=[(T, tb)], inc=[(V, 1)])
            v_za = V.count
            C.emit("gpsimd", (lambda h_: lambda e: e.dma_start(
                out=zscr[:, 4096 * h_:4096 * (h_ + 1)], in_=RA[:, 0:4096]))(sc % 2),
                wait=[(V, v_za), (GP, GP.count)], inc=[(GP, 16)])

        def emit_ffn_pair(pair):
            TP = 512
            xnf = XN[:]
            r_bf = RA[:].bitcast(BF16)
            C.barrier()
            gp_tile = []
            zu = [zscr[:, 4096 * h:4096 * (h + 1)].rearrange(
                "(u c) (b t) -> c b u t", u=8, c=16, b=16) for h in range(2)]
            for c in range(16):
                for h in range(2):
                    C.emit("gpsimd", (lambda c_, h_: lambda e: e.dma_start(
                        out=RB[:, TP * c_ + 256 * h_:TP * c_ + 256 * (h_ + 1)],
                        in_=zu[h_][c_]))(c, h),
                        wait=[(GP, GP.count)] if c == 0 and h == 0 else [], inc=[(GP, 16)])
                gp_tile.append(GP.count)
            gp_zb = GP.count
            for c in range(16):
                pb = ps[2 + c % 2]
                C.emit("tensor", (lambda c_, pb_: lambda e: e.matmul(
                    pb_[:, 0:TP], h128[:], RB[:, TP * c_:TP * (c_ + 1)],
                    start=True, stop=True))(c, pb),
                    wait=[(GP, gp_tile[c])] + ([(V, V.count)] if c > 0 else []), inc=[(T, 1)])
                tb = T.count
                C.emit("vector", (lambda c_, pb_: lambda e: e.tensor_copy(
                    RA[:, TP * c_:TP * (c_ + 1)], pb_[:, 0:TP]))(c, pb),
                    wait=[(T, tb)], inc=[(V, 1)])
            v_xh = V.count
            t_sb = T.count
            r1T = xn_bf[:, 0:8192]
            C.emit("scalar", lambda e: e.activation(
                out=RC[:, 0:8192], in_=RA[:, 0:8192], func=ACT.Abs),
                wait=[(V, v_xh)], inc=[(S, 1)])
            s_abs = S.count
            C.emit("vector", lambda e: e.tensor_tensor(
                out=RB[:, 0:4096], in0=RC[:, 0:4096], in1=RC[:, 4096:8192], op=ALU.max),
                wait=[(S, s_abs), (T, t_sb)], inc=[(V, 1)])
            C.emit("vector", lambda e: e.tensor_tensor(
                out=RB[:, 4096:6144], in0=RB[:, 0:2048], in1=RB[:, 2048:4096], op=ALU.max),
                inc=[(V, 1)])
            C.emit("vector", lambda e: e.tensor_tensor(
                out=RB[:, 0:1024], in0=RB[:, 4096:5120], in1=RB[:, 5120:6144], op=ALU.max),
                inc=[(V, 1)])
            C.emit("vector", lambda e: e.tensor_tensor(
                out=RB[:, 6144:6656], in0=RB[:, 0:512], in1=RB[:, 512:1024], op=ALU.max),
                inc=[(V, 1)])
            v_fold = V.count

            def mrow_chain(src_off, m_col, t):
                C.emit("tensor", (lambda t_, so_: lambda e: e.transpose(
                    ps[1][:, 0:128], RB[:, so_ + 128 * t_:so_ + 128 * (t_ + 1)], ident[:]))(t, src_off),
                    wait=[(V, V.count), (S, S.count)], inc=[(T, 1)])
                C.emit("vector", lambda e: e.tensor_reduce(
                    out=stats[:, 24:25], in_=ps[1][:, 0:128], axis=AX.X, op=ALU.max),
                    wait=[(T, T.count), (V, V.count)], inc=[(V, 1)])
                C.emit("vector", (lambda t_, mc_: lambda e: e.tensor_scalar(
                    out=stats[:, mc_ + t_:mc_ + t_ + 1], in0=stats[:, 24:25], scalar1=1e-5,
                    scalar2=None, op0=ALU.max))(t, m_col), wait=[(V, V.count)], inc=[(V, 1)])
                C.emit("vector", (lambda t_, mc_: lambda e: e.reciprocal(
                    stats[:, 25:26], stats[:, mc_ + t_:mc_ + t_ + 1]))(t, m_col),
                    wait=[(V, V.count)], inc=[(V, 1)])
                C.emit("vector", lambda e: e.tensor_scalar(
                    out=stats[:, 26:27], in0=stats[:, 25:26], scalar1=127.0, scalar2=None,
                    op0=ALU.mult), wait=[(V, V.count)], inc=[(V, 1)])
                C.emit("tensor", lambda e: e.transpose(
                    ps[3][0:1, 0:128], stats[:, 26:27], ident[:]),
                    wait=[(V, V.count)], inc=[(T, 1)])
                C.emit("vector", (lambda t_: lambda e: e.tensor_copy(
                    aux[0:1, 256 + 128 * t_:256 + 128 * (t_ + 1)], ps[3][0:1, 0:128]))(t),
                    wait=[(T, T.count)], inc=[(V, 1)])

            for t in range(4):
                mrow_chain(6144, 12, t)
            C.emit("tensor", lambda e: e.matmul(
                ps[4][:, 0:TP], ones[0:1, 0:128], aux[0:1, 256:256 + TP],
                start=True, stop=True), wait=[(V, V.count)], inc=[(T, 1)])
            t_bc = T.count
            for c in range(16):
                C.emit("vector", (lambda c_: lambda e: e.tensor_tensor(
                    out=RC[:, TP * c_:TP * (c_ + 1)],
                    in0=RA[:, TP * c_:TP * (c_ + 1)],
                    in1=ps[4][:, 0:TP], op=ALU.mult))(c),
                    wait=[(T, t_bc)] if c == 0 else [], inc=[(V, 1)])
            C.emit("vector", lambda e: e.tensor_scalar(
                out=RB[:, 0:8192], in0=RC[:, 0:8192], scalar1=384.5, scalar2=None,
                op0=ALU.add), inc=[(V, 1)])
            C.emit("vector", lambda e: e.tensor_single_scalar(
                out=RC[:, 0:8192].bitcast(I32), in_=RB[:, 0:8192].bitcast(I32),
                scalar=-32768, op=ALU.bitwise_and), inc=[(V, 1)])
            C.emit("vector", lambda e: e.tensor_scalar(
                out=r1T, in0=RC[:, 0:8192], scalar1=-384.0, scalar2=None,
                op0=ALU.add), inc=[(V, 1)])
            v_r1 = V.count
            tku = []
            vr_og = []
            sq_og = []
            for og in range(4):
                for k in range(16):
                    slot = (k % 2) * 1024
                    src = wupT_scr.rearrange("(q c) o -> c q o", c=16)[k, :, 1024 * og:1024 * (og + 1)]
                    ki = og * 16 + k
                    C.emit("sync", (lambda s_, src_: lambda e: e.dma_start(
                        out=wub[:, s_:s_ + 1024], in_=src_))(slot, src),
                        wait=[(T, tku[ki - 2])] if ki >= 2 else [(V, v_r1), (GP, gp_wdone)], inc=[(LD, 16)])
                    ldu = LD.count
                    for m in range(8):
                        kw = []
                        if m == 0:
                            kw = [(LD, ldu)]
                            if k == 0:
                                kw += [(V, v_r1)] if og == 0 else [(V, vr_og[og - 1])]
                        C.emit("tensor", (lambda k_, m_, s_: lambda e: e.matmul(
                            ps[m_][:, 0:TP], wub[:, s_ + 128 * m_:s_ + 128 * (m_ + 1)],
                            r1T[:, TP * k_:TP * (k_ + 1)],
                            start=(k_ == 0), stop=(k_ == 15)))(k, m, slot),
                            wait=kw,
                            inc=([(T, 1)] if m == 7 else []))
                    tku.append(T.count)
                t_up = T.count
                half = RA if og < 2 else RB
                base = 4096 * (og % 2)
                rcb = 4096 * (og % 2)
                for m in range(8):
                    rw = [(T, t_up)] if m == 0 else []
                    if m == 0 and og >= 2:
                        rw += [(S, sq_og[og - 2])]
                    C.emit("vector", (lambda m_, rb_: lambda e: e.tensor_scalar(
                        out=RC[:, rb_ + TP * m_:rb_ + TP * (m_ + 1)], in0=ps[m_][:, 0:TP],
                        scalar1=0.0, scalar2=None, op0=ALU.max))(m, rcb),
                        wait=rw, inc=[(V, 1)])
                vr_og.append(V.count)
                for m in range(8):
                    C.emit("scalar", (lambda m_, h_, b_, rb_: lambda e: e.square(
                        h_[:, b_ + TP * m_:b_ + TP * (m_ + 1)],
                        RC[:, rb_ + TP * m_:rb_ + TP * (m_ + 1)]))(m, half, base, rcb),
                        wait=[(V, vr_og[-1])] if m == 0 else [], inc=[(S, 1)])
                sq_og.append(S.count)
            s_h1 = S.count
            for hb in range(2):
                for b in range(16):
                    idx = hb * 16 + b
                    half = RA if idx < 16 else RB
                    base = TP * (idx % 16)
                    pb = ps[b % 2]
                    C.emit("tensor", (lambda h_, ba_, pb_: lambda e: e.matmul(
                        pb_[:, 0:TP], badn[:], h_[:, ba_:ba_ + TP],
                        start=True, stop=True))(half, base, pb),
                        wait=[(GP, GP.count), (S, s_h1), (V, V.count)] if b == 0 else [(V, V.count)],
                        inc=[(T, 1)])
                    tb = T.count
                    C.emit("vector", (lambda b_, pb_: lambda e: e.tensor_copy(
                        RC[:, TP * b_:TP * (b_ + 1)], pb_[:, 0:TP]))(b, pb),
                        wait=[(T, tb)], inc=[(V, 1)])
                v_za2 = V.count
                C.emit("gpsimd", (lambda h_: lambda e: e.dma_start(
                    out=zscr[:, 8192 * h_:8192 * (h_ + 1)], in_=RC[:, 0:8192]))(hb),
                    wait=[(V, v_za2), (GP, GP.count)], inc=[(GP, 16)])
            zt = zscr[:, 0:16384].rearrange("(u c) (b t) -> c b u t", u=4, c=32, b=32)
            gp_tile2 = []
            for c in range(32):
                half = RA if c < 16 else RB
                C.emit("gpsimd", (lambda c_, h_: lambda e: e.dma_start(
                    out=h_[:, 512 * (c_ % 16):512 * (c_ % 16 + 1)], in_=zt[c_]))(c, half),
                    wait=[(GP, GP.count)] if c == 0 else [], inc=[(GP, 16)])
                gp_tile2.append(GP.count)
            gp_zb2 = GP.count
            def xh2_dst(c):
                if c < 16:
                    return RC[:, TP * c:TP * (c + 1)]
                if c < 24:
                    return SX[:, TP * (c - 16):TP * (c - 15)]
                return xnf[:, TP * (c - 24):TP * (c - 23)]
            for c in range(32):
                pb = ps[2 + c % 2]
                src_half = RA if c < 16 else RB
                C.emit("tensor", (lambda c_, pb_, sh_: lambda e: e.matmul(
                    pb_[:, 0:TP], h128[:], sh_[:, TP * (c_ % 16):TP * (c_ % 16 + 1)],
                    start=True, stop=True))(c, pb, src_half),
                    wait=[(GP, gp_tile2[c])] + ([(V, V.count)] if c > 0 else []), inc=[(T, 1)])
                tb = T.count
                C.emit("vector", (lambda c_, pb_: lambda e: e.tensor_copy(
                    xh2_dst(c_), pb_[:, 0:TP]))(c, pb),
                    wait=[(T, tb)], inc=[(V, 1)])
            v_xh2 = V.count
            t_sb2 = T.count
            H0, H1, H2, H3 = 6144, 6656, 7168, 7680

            def tree512(src, width, op, dst_off):
                w = width // 2
                C.emit("vector", (lambda w_, s_: lambda e: e.tensor_tensor(
                    out=RB[:, 0:w_] if w_ > 512 else RB[:, dst_off:dst_off + 512],
                    in0=s_[:, 0:w_], in1=s_[:, w_:2 * w_], op=op))(w, src), inc=[(V, 1)])
                cur, nxt = 0, 4096
                while w > 512:
                    w //= 2
                    dsto = nxt if w > 512 else dst_off
                    C.emit("vector", (lambda c_, d_, w_: lambda e: e.tensor_tensor(
                        out=RB[:, d_:d_ + w_], in0=RB[:, c_:c_ + w_], in1=RB[:, c_ + w_:c_ + 2 * w_],
                        op=op))(cur, dsto, w), inc=[(V, 1)])
                    cur, nxt = nxt, cur

            def region_absmax(src, width, dst_off, tmp1, tmp2):
                tree512(src, width, ALU.max, tmp1)
                tree512(src, width, ALU.min, dst_off)
                C.emit("vector", (lambda d_, t_: lambda e: e.tensor_scalar(
                    out=RB[:, t_:t_ + 512], in0=RB[:, d_:d_ + 512], scalar1=-1.0, scalar2=None,
                    op0=ALU.mult))(dst_off, tmp2), inc=[(V, 1)])
                C.emit("vector", (lambda d_, t1_, t2_: lambda e: e.tensor_tensor(
                    out=RB[:, d_:d_ + 512], in0=RB[:, t1_:t1_ + 512], in1=RB[:, t2_:t2_ + 512],
                    op=ALU.max))(dst_off, tmp1, tmp2), inc=[(V, 1)])

            C.emit("vector", None, wait=[(V, v_xh2), (T, t_sb2)])
            region_absmax(RC[:, 0:8192], 8192, H0, H1, H2)
            region_absmax(SX[:, 0:4096], 4096, H1, H2, H3)
            C.emit("vector", lambda e: e.tensor_tensor(
                out=RB[:, H2:H2 + 512], in0=RB[:, H0:H0 + 512], in1=RB[:, H1:H1 + 512],
                op=ALU.max), inc=[(V, 1)])
            region_absmax(xnf[:, 0:4096], 4096, H0, H1, H3)
            C.emit("vector", lambda e: e.tensor_tensor(
                out=RB[:, H3:H3 + 512], in0=RB[:, H2:H2 + 512], in1=RB[:, H0:H0 + 512],
                op=ALU.max), inc=[(V, 1)])
            for t in range(4):
                mrow_chain(H3, 16, t)
            C.emit("tensor", lambda e: e.matmul(
                ps[4][:, 0:TP], ones[0:1, 0:128], aux[0:1, 256:256 + TP],
                start=True, stop=True), wait=[(V, V.count)], inc=[(T, 1)])
            t_bc2 = T.count
            r2T = r_bf[:, 0:16384]

            def scale_region(src, n_tiles, bf_off, first):
                for c in range(n_tiles):
                    C.emit("vector", (lambda c_, s_: lambda e: e.tensor_tensor(
                        out=RB[:, TP * c_:TP * (c_ + 1)],
                        in0=s_[:, TP * c_:TP * (c_ + 1)],
                        in1=ps[4][:, 0:TP], op=ALU.mult))(c, src),
                        wait=[(T, t_bc2)] if (first and c == 0) else [], inc=[(V, 1)])
                w = TP * n_tiles
                C.emit("vector", (lambda w_: lambda e: e.tensor_scalar(
                    out=RC[:, 0:w_], in0=RB[:, 0:w_], scalar1=384.5, scalar2=None,
                    op0=ALU.add))(w), inc=[(V, 1)])
                C.emit("vector", (lambda w_: lambda e: e.tensor_single_scalar(
                    out=RB[:, 0:w_].bitcast(I32), in_=RC[:, 0:w_].bitcast(I32),
                    scalar=-32768, op=ALU.bitwise_and))(w), inc=[(V, 1)])
                C.emit("vector", (lambda w_, bo_: lambda e: e.tensor_scalar(
                    out=r2T[:, bo_:bo_ + w_], in0=RB[:, 0:w_], scalar1=-384.0, scalar2=None,
                    op0=ALU.add))(w, bf_off), inc=[(V, 1)])

            scale_region(RC[:, 0:8192], 16, 0, True)
            scale_region(SX[:, 0:4096], 8, 8192, False)
            scale_region(xnf[:, 0:4096], 8, 12288, False)
            v_r2 = V.count
            for t in range(4):
                C.emit("vector", (lambda t_: lambda e: e.tensor_scalar(
                    out=stats[:, 24:25], in0=stats[:, 12 + t_:13 + t_], scalar1=stats[:, 8:9],
                    scalar2=1.0 / 127.0, op0=ALU.mult, op1=ALU.mult))(t),
                    wait=[(V, V.count)], inc=[(V, 1)])
                C.emit("scalar", lambda e: e.square(stats[:, 26:27], stats[:, 24:25]),
                       wait=[(V, V.count)], inc=[(S, 1)])
                C.emit("vector", (lambda t_: lambda e: e.tensor_scalar(
                    out=stats[:, 25:26], in0=stats[:, 16 + t_:17 + t_], scalar1=stats[:, 10:11],
                    scalar2=1.0 / 127.0, op0=ALU.mult, op1=ALU.mult))(t),
                    wait=[(S, S.count), (V, V.count)], inc=[(V, 1)])
                C.emit("vector", (lambda t_: lambda e: e.tensor_tensor(
                    out=stats[:, 20 + t_:21 + t_], in0=stats[:, 26:27], in1=stats[:, 25:26],
                    op=ALU.mult))(t), wait=[(V, V.count)], inc=[(V, 1)])
            v_f = V.count
            for p2 in range(2):
                tkd = []
                for k in range(32):
                    slot = (k % 2) * 1024
                    src = wdnT_scr.rearrange("(q c) o -> c q o", c=32)[k, :, 1024 * p2:1024 * (p2 + 1)]
                    C.emit("sync", (lambda s_, src_: lambda e: e.dma_start(
                        out=wdb[:, s_:s_ + 1024], in_=src_))(slot, src),
                        wait=[(T, tkd[k - 2])] if k >= 2 else ([(V, v_f), (GP, gp_wdone)] if p2 == 0 else [(V, V.count)]),
                        inc=[(LD, 16)])
                    ldd = LD.count
                    for mc in range(4):
                        for n in range(2):
                            C.emit("tensor", (lambda k_, mc_, n_, s_: lambda e: e.matmul(
                                ps[mc_ * 2 + n_][:, 0:TP], r2T[:, TP * k_ + 128 * mc_:TP * k_ + 128 * (mc_ + 1)],
                                wdb[:, s_ + 512 * n_:s_ + 512 * (n_ + 1)],
                                start=(k_ == 0), stop=(k_ == 31)))(k, mc, n, slot),
                                wait=([(LD, ldd)] if (mc == 0 and n == 0) else []) + ([(V, V.count)] if k == 0 and mc == 0 and n == 0 else []),
                                inc=([(T, 1)] if (mc == 3 and n == 1) else []))
                    tkd.append(T.count)
                t_dn = T.count
                for mc in range(4):
                    for n in range(2):
                        C.emit("vector", (lambda mc_, n_, p_: lambda e: e.tensor_scalar(
                            out=RC[:, 2048 * mc_ + 1024 * p_ + 512 * n_:2048 * mc_ + 1024 * p_ + 512 * (n_ + 1)],
                            in0=ps[mc_ * 2 + n_][:, 0:TP], scalar1=stats[:, 20 + mc_:21 + mc_],
                            scalar2=None, op0=ALU.mult))(mc, n, p2),
                            wait=[(T, t_dn), (V, V.count)] if (mc == 0 and n == 0) else [], inc=[(V, 1)])
            t_dn_all = T.count
            v_dn = V.count
            for mc in range(4):
                dst = SX[:, 2048 * mc:2048 * (mc + 1)] if mc < 2 else xnf[:, 2048 * (mc - 2):2048 * (mc - 1)]
                C.emit("scalar", (lambda mc_, d_: lambda e: e.activation(
                    out=d_, in_=RC[:, 2048 * mc_:2048 * (mc_ + 1)], func=ACT.Silu))(mc, dst),
                    wait=[(V, v_dn)] if mc == 0 else [], inc=[(S, 1)])
            s_g = S.count
            for i in range(4):
                xrow = HALO + TP * pair + 128 * i
                C.emit("sync", (lambda i_, r0: lambda e: e.dma_start(
                    out=RB[:, 2048 * i_:2048 * (i_ + 1)], in_=x_in[r0:r0 + 128, :]))(i, xrow),
                    wait=[(T, t_dn_all)] if i == 0 else [], inc=[(LD, 16)])
            ld_x2 = LD.count
            C.emit("gpsimd", (lambda p_: lambda e: e.dma_start(
                out=RA[:, 0:8192], in_=s_scr[:, 8192 * p_:8192 * (p_ + 1)]))(pair),
                wait=[(T, t_dn_all), (GP, GP.count)], inc=[(GP, 16)])
            gp_s = GP.count
            for i in range(4):
                C.emit("vector", (lambda i_: lambda e: e.tensor_tensor(
                    out=RC[:, 2048 * i_:2048 * (i_ + 1)],
                    in0=SX[:, 2048 * i_:2048 * (i_ + 1)] if i_ < 2 else xnf[:, 2048 * (i_ - 2):2048 * (i_ - 1)],
                    in1=RA[:, 2048 * i_:2048 * (i_ + 1)], op=ALU.mult))(i),
                    wait=[(S, s_g), (LD, ld_x2), (GP, gp_s)] if i == 0 else [], inc=[(V, 1)])
                C.emit("vector", (lambda i_: lambda e: e.tensor_tensor(
                    out=RA[:, 2048 * i_:2048 * (i_ + 1)], in0=RB[:, 2048 * i_:2048 * (i_ + 1)],
                    in1=RC[:, 2048 * i_:2048 * (i_ + 1)], op=ALU.add))(i),
                    inc=[(V, 1)])
                vres = V.count
                row0 = TP * pair + 128 * i
                C.emit("gpsimd", (lambda i_, r0: lambda e: e.dma_start(
                    out=out_hbm[r0:r0 + 128, :], in_=RA[:, 2048 * i_:2048 * (i_ + 1)]))(i, row0),
                    wait=[(V, vres)], inc=[(GP, 16)])

        for _pair in range(2):
            emit_attn_half(2 * _pair)
            emit_attn_half(2 * _pair + 1)
            emit_ffn_pair(_pair)

        C.emit("gpsimd", None, wait=[(GP, GP.count)])

        with nc.Block() as block:
            @block.sync
            def _(e):
                C.replay("sync", e)

            @block.gpsimd
            def _(e):
                C.replay("gpsimd", e)

            @block.tensor
            def _(e):
                C.replay("tensor", e)

            @block.vector
            def _(e):
                C.replay("vector", e)

            @block.scalar
            def _(e):
                C.replay("scalar", e)

    return nc


_NC_CACHE = None


def kernel(L0_attn_x, L0_attn_Wq1, L0_attn_Wq2, L0_attn_Wk1, L0_attn_Wk2,
           L0_attn_Wv, L0_attn_Wo, L0_attn_norm_w, L0_ffn_w_up, L0_ffn_w_down):
    global _NC_CACHE
    x = np.ascontiguousarray(np.asarray(L0_attn_x)[0], dtype=np.float32)
    H16, H32, H128 = _hadamard(16), _hadamard(32), _hadamard(128)
    ba_up = (np.kron(np.eye(8, dtype=np.float32), H16) * np.float32(1.0 / np.sqrt(2048.0))).astype(np.float32)
    ba_dn = (np.kron(np.eye(4, dtype=np.float32), H32) * np.float32(2.0 ** -6)).astype(np.float32)
    ident = np.eye(128, dtype=np.float32)
    common = {
        "wq1": np.ascontiguousarray(L0_attn_Wq1, np.float32),
        "wq2": np.ascontiguousarray(L0_attn_Wq2, np.float32),
        "wk1": np.ascontiguousarray(L0_attn_Wk1, np.float32),
        "wk2": np.ascontiguousarray(L0_attn_Wk2, np.float32),
        "wv": np.ascontiguousarray(L0_attn_Wv, np.float32),
        "wo": np.ascontiguousarray(L0_attn_Wo, np.float32),
        "norm_w": np.ascontiguousarray(np.asarray(L0_attn_norm_w, np.float32).reshape(32, 128).T),
        "w_up": np.ascontiguousarray(L0_ffn_w_up, np.float32),
        "w_down": np.ascontiguousarray(L0_ffn_w_down, np.float32),
        "ident": ident, "ba_up": ba_up, "ba_dn": ba_dn, "h128": H128,
    }
    in_maps = []
    for c in range(8):
        lo = S_CORE * c
        halo = np.zeros((HALO, E), np.float32) if c == 0 else x[lo - HALO:lo]
        x_pad = np.ascontiguousarray(np.concatenate([halo, x[lo:lo + S_CORE]], 0))
        in_maps.append({"x_pad": x_pad, **common})
    if _NC_CACHE is None:
        _NC_CACHE = build_nc()
    res = run_bass_kernel_spmd(_NC_CACHE, in_maps, core_ids=list(range(8)))
    out = np.concatenate([res.results[c]["out"] for c in range(8)], 0)
    return out[None].astype(np.float32)


# version 21
# speedup vs baseline: 1.0202x; 1.0202x over previous
import numpy as np
import concourse.bass as bass
import concourse.mybir as mybir
from concourse.bass_utils import run_bass_kernel_spmd

F32 = mybir.dt.float32
I32 = mybir.dt.int32
BF16 = mybir.dt.bfloat16
ALU = mybir.AluOpType
ACT = mybir.ActivationFunctionType
AX = mybir.AxisListType

E = 2048
E2 = 4096
HD = 128
TO = 256
HALO = 128
NSC = 4
S_CORE = NSC * TO
S_PAD = S_CORE + HALO


def _hadamard(n):
    H = np.array([[1.0]], dtype=np.float32)
    while H.shape[0] < n:
        H = np.block([[H, H], [H, -H]]).astype(np.float32)
    return H


class _Sem:
    __slots__ = ("name", "count", "h")

    def __init__(self, name):
        self.name = name
        self.count = 0
        self.h = None


class _Ctx:

    def __init__(self, nc):
        self.nc = nc
        self.streams = {k: [] for k in ("sync", "gpsimd", "tensor", "vector", "scalar")}
        self.sems = {k: _Sem(k) for k in ("LD", "GP", "T", "V", "S", "L2")}

    def emit(self, eng, fn, wait=(), inc=()):
        wait = [(s, v) for (s, v) in wait if v > 0]
        self.streams[eng].append((wait, fn, list(inc)))
        for s, n in inc:
            s.count += n

    def barrier(self):
        sems = list(self.sems.values())
        waits = [(x, x.count) for x in sems]
        for eng in self.streams:
            self.emit(eng, None, wait=list(waits))

    def replay(self, eng_name, e):
        for wait, fn, inc in self.streams[eng_name]:
            for s, v in wait:
                e.wait_ge(s.h, v)
            if fn is None:
                continue
            inst = fn(e)
            for s, n in inc:
                inst.then_inc(s.h, n)


def build_nc():
    nc = bass.Bass(trn_type="TRN2", target_bir_lowering=False, detect_race_conditions=False)

    x_in = nc.dram_tensor("x_pad", [S_PAD, E], F32, kind="ExternalInput").ap()
    wq1 = nc.dram_tensor("wq1", [E, HD], F32, kind="ExternalInput").ap()
    wq2 = nc.dram_tensor("wq2", [HD, E2], F32, kind="ExternalInput").ap()
    wk1 = nc.dram_tensor("wk1", [E, HD], F32, kind="ExternalInput").ap()
    wk2 = nc.dram_tensor("wk2", [HD, E2], F32, kind="ExternalInput").ap()
    wv = nc.dram_tensor("wv", [E, E2], F32, kind="ExternalInput").ap()
    wo = nc.dram_tensor("wo", [E2, E], F32, kind="ExternalInput").ap()
    nw_in = nc.dram_tensor("norm_w", [128, 32], F32, kind="ExternalInput").ap()
    wup_in = nc.dram_tensor("w_up", [E2, E], F32, kind="ExternalInput").ap()
    wdn_in = nc.dram_tensor("w_down", [E, E2], F32, kind="ExternalInput").ap()
    ident_in = nc.dram_tensor("ident", [128, 128], F32, kind="ExternalInput").ap()
    baup_in = nc.dram_tensor("ba_up", [128, 128], F32, kind="ExternalInput").ap()
    badn_in = nc.dram_tensor("ba_dn", [128, 128], F32, kind="ExternalInput").ap()
    h128_in = nc.dram_tensor("h128", [128, 128], F32, kind="ExternalInput").ap()
    out_hbm = nc.dram_tensor("out", [S_CORE, E], F32, kind="ExternalOutput").ap()

    zscr = nc.dram_tensor("zscr", [128, 16384], F32).ap()
    s_scr = nc.dram_tensor("s_scr", [128, 16384], F32).ap()
    wupT_scr = nc.dram_tensor("wupT_scr", [E, E2], BF16).ap()
    wdnT_scr = nc.dram_tensor("wdnT_scr", [E2, E], BF16).ap()

    from contextlib import ExitStack
    with ExitStack() as st:
        def sb(name, shape, dt=F32):
            return st.enter_context(nc.sbuf_tensor(name, shape, dt))

        wq1_sb = sb("wq1_sb", [128, E])
        wk1_sb = sb("wk1_sb", [128, E])
        nw_sb = sb("nw_sb", [128, 32])
        state = sb("state", [128, 32])
        ident = sb("ident_sb", [128, 128])
        baup = sb("baup_sb", [128, 128])
        badn = sb("badn_sb", [128, 128])
        h128 = sb("h128_sb", [128, 128])
        ones = sb("ones_sb", [128, 128])
        ident_bf = sb("identbf_sb", [128, 128], BF16)
        q1T = sb("q1T", [128, TO])
        lf1T = sb("lf1T", [128, 384])
        stats = sb("stats", [128, 32])
        aux = sb("aux", [128, 768])
        rmsacc = sb("rmsacc", [128, 2 * TO])
        RA = sb("RA", [128, 8192])
        RB = sb("RB", [128, 8192])
        RC = sb("RC", [128, 8192])
        SX = sb("SX", [128, 2 * E])
        XN = sb("XN", [128, 16 * 384])
        wvb = sb("wvb", [128, 2048])
        wqg = sb("wqg", [128, 1024])
        wkg = sb("wkg", [128, 1024])
        wob = sb("wob", [128, 4096])
        wub = sb("wub", [128, 2048], BF16)
        wdb = sb("wdb", [128, 4096], BF16)

        ps = [st.enter_context(nc.psum_tensor(f"ps{i}", [128, 512], F32)) for i in range(8)]

        C = _Ctx(nc)
        for s in C.sems.values():
            s.h = st.enter_context(nc.semaphore(s.name))
        LD, GP, T, V, S, L2 = (C.sems[k] for k in ("LD", "GP", "T", "V", "S", "L2"))

        xn_bf = XN[:].bitcast(BF16)

        C.emit("sync", lambda e: e.dma_start(out=ident[:], in_=ident_in), inc=[(LD, 16)])
        C.emit("sync", lambda e: e.dma_start(out=baup[:], in_=baup_in), inc=[(LD, 16)])
        C.emit("sync", lambda e: e.dma_start(out=badn[:], in_=badn_in), inc=[(LD, 16)])
        C.emit("sync", lambda e: e.dma_start(out=h128[:], in_=h128_in), inc=[(LD, 16)])
        C.emit("sync", lambda e: e.dma_start(out=nw_sb[:, 0:32], in_=nw_in), inc=[(LD, 16)])
        C.emit("sync", lambda e: e.dma_start(
            out=wq1_sb[:].rearrange("p (k m) -> p k m", k=16),
            in_=wq1.rearrange("(k p) m -> p k m", p=128)), inc=[(LD, 16)])
        C.emit("sync", lambda e: e.dma_start(
            out=wk1_sb[:].rearrange("p (k m) -> p k m", k=16),
            in_=wk1.rearrange("(k p) m -> p k m", p=128)), inc=[(LD, 16)])
        C.emit("vector", lambda e: e.memset(state[:], 0.0), inc=[(V, 1)])
        C.emit("vector", lambda e: e.memset(ones[:], 1.0), inc=[(V, 1)])

        wacc = aux
        red_u, red_d, ld_u, ld_d = [], [], [], []
        for i in range(32):
            slot = (i % 2) * 4096
            src = wup_in[128 * i:128 * (i + 1), :]
            C.emit("sync", (lambda s_, src_: lambda e: e.dma_start(out=RA[:, s_:s_ + 2048], in_=src_))(slot, src),
                   wait=[(V, red_u[i - 2])] if i >= 2 else [], inc=[(LD, 16)])
            ld_u.append(LD.count)
            C.emit("vector", (lambda s_, i_: lambda e: e.tensor_reduce(
                out=wacc[:, i_:i_ + 1], in_=RA[:, s_:s_ + 2048], axis=AX.X, op=ALU.add,
                apply_absolute_value=True))(slot, i),
                wait=[(LD, ld_u[-1])], inc=[(V, 1)])
            red_u.append(V.count)
            if i % 2 == 1 and i // 2 < 16:
                j = i // 2
                slot2 = (j % 2) * 4096
                src2 = wdn_in[128 * j:128 * (j + 1), :]
                C.emit("scalar", (lambda s_, src_: lambda e: e.dma_start(out=RB[:, s_:s_ + 4096], in_=src_))(slot2, src2),
                       wait=[(V, red_d[j - 2])] if j >= 2 else [], inc=[(L2, 16)])
                ld_d.append(L2.count)
                C.emit("vector", (lambda s_, j_: lambda e: e.tensor_reduce(
                    out=wacc[:, 32 + j_:33 + j_], in_=RB[:, s_:s_ + 4096], axis=AX.X, op=ALU.add,
                    apply_absolute_value=True))(slot2, j),
                    wait=[(L2, ld_d[-1])], inc=[(V, 1)])
                red_d.append(V.count)
        C.emit("vector", lambda e: e.tensor_reduce(out=stats[:, 21:22], in_=wacc[:, 0:32],
                                                   axis=AX.X, op=ALU.add), inc=[(V, 1)])
        C.emit("vector", lambda e: e.tensor_reduce(out=stats[:, 22:23], in_=wacc[:, 32:48],
                                                   axis=AX.X, op=ALU.add), inc=[(V, 1)])
        vham = V.count
        C.emit("tensor", lambda e: e.matmul(ps[0][0:1, 0:2], ones[:, 0:1], stats[:, 21:23],
                                            start=True, stop=True),
               wait=[(V, vham)], inc=[(T, 1)])
        C.emit("vector", lambda e: e.tensor_copy(aux[0:1, 0:2], ps[0][0:1, 0:2]),
               wait=[(T, T.count)], inc=[(V, 1)])
        C.emit("tensor", lambda e: e.matmul(ps[1][:, 0:2], ones[0:1, 0:128], aux[0:1, 0:2],
                                            start=True, stop=True),
               wait=[(V, V.count)], inc=[(T, 1)])
        C.emit("vector", lambda e: e.tensor_scalar(out=stats[:, 8:9], in0=ps[1][:, 0:1],
                                                   scalar1=1.0 / (4096.0 * 2048.0), scalar2=1e-5,
                                                   op0=ALU.mult, op1=ALU.max),
               wait=[(T, T.count)], inc=[(V, 1)])
        C.emit("vector", lambda e: e.tensor_scalar(out=stats[:, 10:11], in0=ps[1][:, 1:2],
                                                   scalar1=1.0 / (2048.0 * 4096.0), scalar2=1e-5,
                                                   op0=ALU.mult, op1=ALU.max), inc=[(V, 1)])
        C.emit("vector", lambda e: e.reciprocal(stats[:, 9:10], stats[:, 8:9]), inc=[(V, 1)])
        C.emit("vector", lambda e: e.reciprocal(stats[:, 11:12], stats[:, 10:11]), inc=[(V, 1)])
        vscales = V.count

        ra_bf = RA[:].bitcast(BF16)
        C.emit("scalar", lambda e: e.copy(out=ident_bf[:], in_=ident[:]),
               wait=[(LD, 16)], inc=[(S, 1)])
        C.barrier()

        blk_hist = []
        tp_done = []
        tp_grp = []
        stg = RC[:].bitcast(BF16)

        def w_quant_pass(win, nblk, ncols, rws_col, scrT, ktiles):
            for i in range(nblk):
                g = len(blk_hist)
                slot = (i % 2) * 4096
                src = win[128 * i:128 * (i + 1), :]
                dw = []
                if g >= 2:
                    h = blk_hist[g - 2]
                    dw = [(V, h["cmp"]), (T, h["tdone"]), (GP, h["gp"])]
                C.emit("sync", (lambda s_, src_, nc_: lambda e: e.dma_start(
                    out=RA[:, s_:s_ + nc_], in_=src_))(slot, src, ncols),
                    wait=dw, inc=[(LD, 16)])
                ldw = LD.count
                C.emit("vector", (lambda s_, nc_: lambda e: e.tensor_scalar(
                    out=RB[:, s_:s_ + nc_], in0=RA[:, s_:s_ + nc_], scalar1=stats[:, rws_col:rws_col + 1],
                    scalar2=0.5, op0=ALU.mult, op1=ALU.is_gt))(slot, ncols),
                    wait=[(LD, ldw), (V, max(0, vscales))], inc=[(V, 1)])
                C.emit("vector", (lambda s_, nc_: lambda e: e.tensor_scalar(
                    out=RC[:, s_:s_ + nc_], in0=RA[:, s_:s_ + nc_], scalar1=stats[:, rws_col:rws_col + 1],
                    scalar2=-0.5, op0=ALU.mult, op1=ALU.is_lt))(slot, ncols), inc=[(V, 1)])
                C.emit("vector", (lambda s_, nc_: lambda e: e.tensor_tensor(
                    out=ra_bf[:, 2 * s_:2 * s_ + nc_], in0=RB[:, s_:s_ + nc_], in1=RC[:, s_:s_ + nc_],
                    op=ALU.subtract))(slot, ncols), inc=[(V, 1)])
                vt = V.count
                stg_off = (i % 2) * 8192
                for k in range(ktiles):
                    pk = len(tp_done)
                    grp = pk // 4
                    pb = ps[grp % 8]
                    off = 64 * (pk % 4)
                    C.emit("tensor", (lambda s_, k_, pb_, o_: lambda e: e.transpose(
                        pb_[:, o_:o_ + 64].bitcast(BF16),
                        ra_bf[:, 2 * s_ + 128 * k_:2 * s_ + 128 * (k_ + 1)],
                        ident_bf[:]))(slot, k, pb, off),
                        wait=([(V, vt)] if k == 0 else []) +
                             ([(S, tp_grp[grp - 8])] if (pk % 4 == 0 and grp >= 8) else []),
                        inc=[(T, 1)])
                    tp_done.append(None)
                    if pk % 4 == 3:
                        tc = T.count
                        C.emit("scalar", (lambda so_, k_, pb_: lambda e: e.copy(
                            out=stg[:, so_ + 128 * (k_ - 3):so_ + 128 * (k_ + 1)],
                            in_=pb_[:, 0:256].bitcast(BF16)))(stg_off, k, pb),
                            wait=[(T, tc)] + ([(GP, blk_hist[g - 2]["gp"])] if (k == 3 and g >= 2) else []),
                            inc=[(S, 1)])
                        tp_grp.append(S.count)
                sc = S.count
                dst = scrT.rearrange("(k p) o -> p k o", p=128)[:, :, 128 * i:128 * (i + 1)]
                C.emit("gpsimd", (lambda so_, dst_, k_: lambda e: e.dma_start(
                    out=dst_, in_=stg[:, so_:so_ + 128 * k_].rearrange("p (k o) -> p k o", k=k_)))(stg_off, dst, ktiles),
                    wait=[(S, sc)], inc=[(GP, 16)])
                blk_hist.append({"cmp": vt, "tdone": T.count, "gp": GP.count, "sdone": sc})

        w_quant_pass(wup_in, 32, 2048, 9, wupT_scr, 16)
        w_quant_pass(wdn_in, 16, 4096, 11, wdnT_scr, 32)
        gp_wdone = GP.count

        C.barrier()
        def emit_attn_half(sc):
            ts = 384 if sc == 0 else 256
            ofs = 0 if sc == 0 else 384 + 256 * (sc - 1)
            nt = ts // 128
            oofs = 128 if sc == 0 else 0

            C.barrier()
            for i in range(nt):
                owned_i = i - (1 if sc == 0 else 0)
                xst = 4096
                C.emit("sync", (lambda r0: lambda e: e.dma_start(
                    out=RB[:, xst:xst + 2048], in_=x_in[r0:r0 + 128, :]))(ofs + 128 * i),
                    wait=[(V, V.count), (T, T.count), (GP, GP.count)], inc=[(LD, 16)])
                ldx = LD.count
                C.emit("vector", lambda e: e.tensor_reduce(
                    out=stats[:, 5:6], in_=RB[:, xst:xst + 2048], axis=AX.X, op=ALU.add),
                    wait=[(LD, ldx)], inc=[(V, 1)])
                C.emit("scalar", lambda e: e.activation(
                    out=RC[:, 0:2048], in_=RB[:, xst:xst + 2048], func=ACT.Square,
                    accum_out=stats[:, 6:7]),
                    wait=[(LD, ldx)], inc=[(S, 1)])
                C.emit("vector", lambda e: e.tensor_scalar(
                    out=stats[:, 0:1], in0=stats[:, 5:6], scalar1=1.0 / 2048.0, scalar2=None,
                    op0=ALU.mult), wait=[(S, S.count), (V, V.count)], inc=[(V, 1)])
                C.emit("vector", lambda e: e.tensor_scalar(
                    out=stats[:, 1:2], in0=stats[:, 6:7], scalar1=1.0 / 2048.0, scalar2=1e-5,
                    op0=ALU.mult, op1=ALU.add), inc=[(V, 1)])
                C.emit("scalar", lambda e: e.square(stats[:, 2:3], stats[:, 0:1]),
                       wait=[(V, V.count)], inc=[(S, 1)])
                C.emit("vector", lambda e: e.tensor_tensor(
                    out=stats[:, 7:8], in0=stats[:, 1:2], in1=stats[:, 2:3], op=ALU.subtract),
                    wait=[(S, S.count), (V, V.count)], inc=[(V, 1)])
                C.emit("scalar", lambda e: e.sqrt(stats[:, 3:4], stats[:, 7:8]),
                       wait=[(V, V.count)], inc=[(S, 1)])
                C.emit("vector", lambda e: e.reciprocal(stats[:, 4:5], stats[:, 3:4]),
                       wait=[(S, S.count)], inc=[(V, 1)])
                C.emit("vector", lambda e: e.tensor_scalar(
                    out=RC[:, 4096:6144], in0=RB[:, xst:xst + 2048], scalar1=stats[:, 0:1],
                    scalar2=stats[:, 4:5], op0=ALU.subtract, op1=ALU.mult),
                    wait=[(T, T.count), (V, V.count)], inc=[(V, 1)])
                vxn = V.count
                for k in range(16):
                    pb = ps[k % 2]
                    C.emit("tensor", (lambda k_, pb_: lambda e: e.transpose(
                        pb_[:, 0:128], RC[:, 4096 + 128 * k_:4096 + 128 * (k_ + 1)], ident[:]))(k, pb),
                        wait=[(V, vxn), (S, S.count)], inc=[(T, 1)])
                    tc = T.count
                    C.emit("scalar", (lambda k_, i_, pb_: lambda e: e.copy(
                        out=XN[:, 384 * k_ + 128 * i_:384 * k_ + 128 * (i_ + 1)], in_=pb_[:, 0:128]))(k, i, pb),
                        wait=[(T, tc)], inc=[(S, 1)])
            s_xn = S.count
            for (w1sb, dstT, n0, nlen) in ((wq1_sb, q1T, oofs, TO), (wk1_sb, lf1T, 0, ts)):
                for k in range(16):
                    C.emit("tensor", (lambda k_, w_, n0_, nl_: lambda e: e.matmul(
                        ps[2][:, 0:nl_], w_[:, 128 * k_:128 * (k_ + 1)],
                        XN[:, 384 * k_ + n0_:384 * k_ + n0_ + nl_],
                        start=(k_ == 0), stop=(k_ == 15)))(k, w1sb, n0, nlen),
                        wait=([(S, s_xn), (V, V.count)] if k == 0 else []),
                        inc=([(T, 1)] if k == 15 else []))
                tq = T.count
                C.emit("vector", (lambda d_, nl_: lambda e: e.tensor_copy(d_[:, 0:nl_], ps[2][:, 0:nl_]))(dstT, nlen),
                       wait=[(T, tq)], inc=[(V, 1)])
            v_q1 = V.count
            for g in range(4):
                gw = [(V, v_q1)] if g == 0 else []
                C.emit("sync", (lambda g_: lambda e: e.dma_start(
                    out=wqg[:], in_=wq2[:, 1024 * g_:1024 * (g_ + 1)]))(g),
                    wait=gw + [(T, T.count)], inc=[(LD, 16)])
                C.emit("sync", (lambda g_: lambda e: e.dma_start(
                    out=wkg[:], in_=wk2[:, 1024 * g_:1024 * (g_ + 1)]))(g), inc=[(LD, 16)])
                ld_qk = LD.count
                for m in range(8):
                    C.emit("tensor", (lambda m_: lambda e: e.matmul(
                        ps[m_][:, 0:ts], wkg[:, 128 * m_:128 * (m_ + 1)], lf1T[:, 0:ts],
                        start=True, stop=True))(m),
                        wait=([(LD, ld_qk), (S, S.count), (V, V.count)] if m == 0 else []),
                        inc=[(T, 1)])
                    tlf = T.count
                    C.emit("scalar", (lambda m_, ts_: lambda e: e.activation(
                        out=RC[:, 4096 + ts_ * m_:4096 + ts_ * (m_ + 1)], in_=ps[m_][:, 0:ts_],
                        func=ACT.Sigmoid))(m, ts),
                        wait=[(T, tlf)] + ([(V, V.count)] if m == 0 else []), inc=[(S, 1)])
                    C.emit("scalar", (lambda m_, ts_: lambda e: e.activation(
                        out=RA[:, ts_ * m_:ts_ * (m_ + 1)], in_=RC[:, 4096 + ts_ * m_:4096 + ts_ * (m_ + 1)],
                        func=ACT.Sigmoid))(m, ts),
                        wait=[(GP, GP.count)] if (g == 0 and m == 0) else [], inc=[(S, 1)])
                s_a = S.count
                C.emit("vector", lambda e: e.tensor_scalar(
                    out=RC[:, 4096:4096 + 8 * ts], in0=RA[:, 0:8 * ts], scalar1=-1.0, scalar2=1.0,
                    op0=ALU.mult, op1=ALU.add), wait=[(S, s_a)], inc=[(V, 1)])
                v_k = V.count
                tk = []
                for k in range(16):
                    slot = (k % 2) * 1024
                    C.emit("sync", (lambda g_, k_, s_: lambda e: e.dma_start(
                        out=wvb[:, s_:s_ + 1024], in_=wv[128 * k_:128 * (k_ + 1), 1024 * g_:1024 * (g_ + 1)]))(g, k, slot),
                        wait=[(T, tk[k - 2])] if k >= 2 else [(S, s_a)], inc=[(LD, 16)])
                    ldv = LD.count
                    for m in range(8):
                        C.emit("tensor", (lambda k_, m_, s_: lambda e: e.matmul(
                            ps[m_][:, 0:ts], wvb[:, s_ + 128 * m_:s_ + 128 * (m_ + 1)],
                            XN[:, 384 * k_:384 * k_ + ts],
                            start=(k_ == 0), stop=(k_ == 15)))(k, m, slot),
                            wait=([(LD, ldv)] if m == 0 else []) + ([(V, v_k)] if k == 0 and m == 0 else []),
                            inc=([(T, 1)] if m == 7 else []))
                    tk.append(T.count)
                t_v = T.count
                for m in range(8):
                    C.emit("vector", (lambda m_: lambda e: e.tensor_tensor(
                        out=RB[:, ts * m_:ts * (m_ + 1)], in0=RC[:, 4096 + ts * m_:4096 + ts * (m_ + 1)],
                        in1=ps[m_][:, 0:ts], op=ALU.mult))(m),
                        wait=[(T, t_v)] if m == 0 else [], inc=[(V, 1)])
                    C.emit("vector", (lambda g_, m_: lambda e: e.tensor_tensor_scan(
                        out=RC[:, ts * m_:ts * (m_ + 1)], data0=RA[:, ts * m_:ts * (m_ + 1)],
                        data1=RB[:, ts * m_:ts * (m_ + 1)], initial=state[:, 8 * g_ + m_:8 * g_ + m_ + 1],
                        op0=ALU.mult, op1=ALU.add))(g, m), inc=[(V, 1)])
                    C.emit("vector", (lambda g_, m_: lambda e: e.tensor_copy(
                        state[:, 8 * g_ + m_:8 * g_ + m_ + 1], RC[:, ts * m_ + ts - 1:ts * m_ + ts]))(g, m),
                        wait=[(V, V.count)], inc=[(V, 1)])
                v_scan = V.count
                for m in range(8):
                    C.emit("tensor", (lambda m_: lambda e: e.matmul(
                        ps[m_][:, 0:TO], wqg[:, 128 * m_:128 * (m_ + 1)], q1T[:, 0:TO],
                        start=True, stop=True))(m),
                        wait=([(V, v_scan)] if m == 0 else []), inc=[(T, 1)])
                    tqm = T.count
                    C.emit("scalar", (lambda m_: lambda e: e.activation(
                        out=RA[:, TO * m_:TO * (m_ + 1)], in_=ps[m_][:, 0:TO], func=ACT.Silu))(m),
                        wait=[(T, tqm)], inc=[(S, 1)])
                s_q = S.count
                for m in range(8):
                    C.emit("vector", (lambda m_: lambda e: e.tensor_tensor(
                        out=RB[:, TO * m_:TO * (m_ + 1)], in0=RA[:, TO * m_:TO * (m_ + 1)],
                        in1=RC[:, ts * m_ + oofs:ts * m_ + oofs + TO], op=ALU.mult))(m),
                        wait=[(S, s_q)] if m == 0 else [], inc=[(V, 1)])
                v_out = V.count
                for m in range(8):
                    C.emit("scalar", (lambda m_: lambda e: e.square(
                        RC[:, 4096 + TO * m_:4096 + TO * (m_ + 1)], RB[:, TO * m_:TO * (m_ + 1)]))(m),
                        wait=[(V, v_out)] if m == 0 else [], inc=[(S, 1)])
                s_sq = S.count
                SL = 6144
                for j in range(4):
                    C.emit("vector", (lambda j_: lambda e: e.tensor_tensor(
                        out=RC[:, SL + TO * j_:SL + TO * (j_ + 1)],
                        in0=RC[:, 4096 + TO * (2 * j_):4096 + TO * (2 * j_ + 1)],
                        in1=RC[:, 4096 + TO * (2 * j_ + 1):4096 + TO * (2 * j_ + 2)], op=ALU.add))(j),
                        wait=[(S, s_sq)] if j == 0 else [], inc=[(V, 1)])
                for j in range(2):
                    C.emit("vector", (lambda j_: lambda e: e.tensor_tensor(
                        out=RC[:, 4096 + TO * j_:4096 + TO * (j_ + 1)],
                        in0=RC[:, SL + TO * (2 * j_):SL + TO * (2 * j_ + 1)],
                        in1=RC[:, SL + TO * (2 * j_ + 1):SL + TO * (2 * j_ + 2)], op=ALU.add))(j),
                        inc=[(V, 1)])
                C.emit("vector", lambda e: e.tensor_tensor(
                    out=RC[:, SL:SL + TO], in0=RC[:, 4096:4096 + TO],
                    in1=RC[:, 4096 + TO:4096 + 2 * TO], op=ALU.add), inc=[(V, 1)])
                if g == 0:
                    C.emit("vector", lambda e: e.tensor_copy(rmsacc[:, 0:TO], RC[:, SL:SL + TO]), inc=[(V, 1)])
                else:
                    C.emit("vector", (lambda g_: lambda e: e.tensor_tensor(
                        out=rmsacc[:, TO * (g_ % 2):TO * (g_ % 2) + TO],
                        in0=rmsacc[:, TO * ((g_ - 1) % 2):TO * ((g_ - 1) % 2) + TO],
                        in1=RC[:, SL:SL + TO], op=ALU.add))(g), inc=[(V, 1)])
                for m in range(8):
                    mg = 8 * g + m
                    half, loc = (RA, mg) if mg < 16 else (RB, mg - 16)
                    C.emit("vector", (lambda m_, h_, l_, mg_: lambda e: e.tensor_scalar(
                        out=h_[:, 3072 + TO * l_:3072 + TO * (l_ + 1)], in0=RB[:, TO * m_:TO * (m_ + 1)],
                        scalar1=nw_sb[:, mg_:mg_ + 1], scalar2=None, op0=ALU.mult))(m, half, loc, mg),
                        inc=[(V, 1)])
            v_att = V.count
            C.emit("tensor", lambda e: e.matmul(ps[0][0:1, 0:TO], ones[:, 0:1], rmsacc[:, TO:2 * TO],
                                                start=True, stop=True),
                   wait=[(V, v_att)], inc=[(T, 1)])
            C.emit("vector", lambda e: e.tensor_scalar(
                out=aux[0:1, 0:TO], in0=ps[0][0:1, 0:TO], scalar1=1.0 / 4096.0, scalar2=1e-6,
                op0=ALU.mult, op1=ALU.add), wait=[(T, T.count)], inc=[(V, 1)])
            C.emit("scalar", lambda e: e.sqrt(aux[0:1, 256:256 + TO], aux[0:1, 0:TO]),
                   wait=[(V, V.count)], inc=[(S, 1)])
            C.emit("vector", lambda e: e.reciprocal(aux[0:1, 512:512 + TO], aux[0:1, 256:256 + TO]),
                   wait=[(S, S.count), (V, V.count)], inc=[(V, 1)])
            v_rr = V.count
            for i in range(2):
                C.emit("tensor", (lambda i_: lambda e: e.transpose(
                    ps[1 + i_][:, 0:1], aux[0:1, 512 + 128 * i_:512 + 128 * (i_ + 1)], ident[0:1, 0:1]))(i),
                    wait=[(V, v_rr if i == 0 else V.count)], inc=[(T, 1)])
                C.emit("vector", (lambda i_: lambda e: e.tensor_copy(
                    stats[:, 28 + i_:29 + i_], ps[1 + i_][:, 0:1]))(i),
                    wait=[(T, T.count)], inc=[(V, 1)])
            v_rrms = V.count
            tk = []
            for k in range(32):
                slot = (k % 2) * 2048
                half, loc = (RA, k) if k < 16 else (RB, k - 16)
                C.emit("sync", (lambda k_, s_: lambda e: e.dma_start(
                    out=wob[:, s_:s_ + 2048], in_=wo[128 * k_:128 * (k_ + 1), :]))(k, slot),
                    wait=[(T, tk[k - 2])] if k >= 2 else [(V, v_rrms)], inc=[(LD, 16)])
                ldo = LD.count
                for mc in range(2):
                    for n in range(4):
                        C.emit("tensor", (lambda k_, mc_, n_, s_, h_, l_: lambda e: e.matmul(
                            ps[mc_ * 4 + n_][:, 0:512],
                            h_[:, 3072 + TO * l_ + 128 * mc_:3072 + TO * l_ + 128 * (mc_ + 1)],
                            wob[:, s_ + 512 * n_:s_ + 512 * (n_ + 1)],
                            start=(k_ == 0), stop=(k_ == 31)))(k, mc, n, slot, half, loc),
                            wait=([(LD, ldo)] if (mc == 0 and n == 0) else []),
                            inc=([(T, 1)] if (mc == 1 and n == 3) else []))
                tk.append(T.count)
            t_wo = T.count
            for mc in range(2):
                for n in range(4):
                    C.emit("vector", (lambda mc_, n_: lambda e: e.tensor_scalar(
                        out=SX[:, 2048 * mc_ + 512 * n_:2048 * mc_ + 512 * (n_ + 1)],
                        in0=ps[mc_ * 4 + n_][:, 0:512], scalar1=stats[:, 28 + mc_:29 + mc_],
                        scalar2=None, op0=ALU.mult))(mc, n),
                        wait=[(T, t_wo), (V, V.count)] if (mc == 0 and n == 0) else [], inc=[(V, 1)])
            v_s = V.count
            C.emit("gpsimd", (lambda sc_: lambda e: e.dma_start(
                out=s_scr[:, 4096 * sc_:4096 * (sc_ + 1)], in_=SX[:]))(sc),
                wait=[(V, v_s), (GP, GP.count)], inc=[(GP, 16)])
            for b in range(16):
                pb = ps[b % 2]
                C.emit("tensor", (lambda b_, pb_: lambda e: e.matmul(
                    pb_[:, 0:TO], baup[:], XN[:, 384 * b_ + oofs:384 * b_ + oofs + TO],
                    start=True, stop=True))(b, pb),
                    wait=[(V, v_s)] if b == 0 else [(V, V.count)], inc=[(T, 1)])
                tb = T.count
                C.emit("vector", (lambda b_, pb_: lambda e: e.tensor_copy(
                    RA[:, TO * b_:TO * (b_ + 1)], pb_[:, 0:TO]))(b, pb),
                    wait=[(T, tb)], inc=[(V, 1)])
            v_za = V.count
            C.emit("gpsimd", (lambda h_: lambda e: e.dma_start(
                out=zscr[:, 4096 * h_:4096 * (h_ + 1)], in_=RA[:, 0:4096]))(sc % 2),
                wait=[(V, v_za), (GP, GP.count)], inc=[(GP, 16)])

        def emit_ffn_pair(pair):
            TP = 512
            xnf = XN[:]
            r_bf = RA[:].bitcast(BF16)
            C.barrier()
            gp_tile = []
            zu = [zscr[:, 4096 * h:4096 * (h + 1)].rearrange(
                "(u c) (b t) -> c b u t", u=8, c=16, b=16) for h in range(2)]
            for c in range(16):
                for h in range(2):
                    C.emit("gpsimd", (lambda c_, h_: lambda e: e.dma_start(
                        out=RB[:, TP * c_ + 256 * h_:TP * c_ + 256 * (h_ + 1)],
                        in_=zu[h_][c_]))(c, h),
                        wait=[(GP, GP.count)] if c == 0 and h == 0 else [], inc=[(GP, 16)])
                gp_tile.append(GP.count)
            gp_zb = GP.count
            for c in range(16):
                pb = ps[2 + c % 2]
                C.emit("tensor", (lambda c_, pb_: lambda e: e.matmul(
                    pb_[:, 0:TP], h128[:], RB[:, TP * c_:TP * (c_ + 1)],
                    start=True, stop=True))(c, pb),
                    wait=[(GP, gp_tile[c])] + ([(V, V.count)] if c > 0 else []), inc=[(T, 1)])
                tb = T.count
                C.emit("vector", (lambda c_, pb_: lambda e: e.tensor_copy(
                    RA[:, TP * c_:TP * (c_ + 1)], pb_[:, 0:TP]))(c, pb),
                    wait=[(T, tb)], inc=[(V, 1)])
            v_xh = V.count
            t_sb = T.count
            r1T = xn_bf[:, 0:8192]
            C.emit("scalar", lambda e: e.activation(
                out=RC[:, 0:8192], in_=RA[:, 0:8192], func=ACT.Abs),
                wait=[(V, v_xh)], inc=[(S, 1)])
            s_abs = S.count
            C.emit("vector", lambda e: e.tensor_tensor(
                out=RB[:, 0:4096], in0=RC[:, 0:4096], in1=RC[:, 4096:8192], op=ALU.max),
                wait=[(S, s_abs), (T, t_sb)], inc=[(V, 1)])
            C.emit("vector", lambda e: e.tensor_tensor(
                out=RB[:, 4096:6144], in0=RB[:, 0:2048], in1=RB[:, 2048:4096], op=ALU.max),
                inc=[(V, 1)])
            C.emit("vector", lambda e: e.tensor_tensor(
                out=RB[:, 0:1024], in0=RB[:, 4096:5120], in1=RB[:, 5120:6144], op=ALU.max),
                inc=[(V, 1)])
            C.emit("vector", lambda e: e.tensor_tensor(
                out=RB[:, 6144:6656], in0=RB[:, 0:512], in1=RB[:, 512:1024], op=ALU.max),
                inc=[(V, 1)])
            v_fold = V.count

            def mrow_chain(src_off, m_col, t):
                C.emit("tensor", (lambda t_, so_: lambda e: e.transpose(
                    ps[1][:, 0:128], RB[:, so_ + 128 * t_:so_ + 128 * (t_ + 1)], ident[:]))(t, src_off),
                    wait=[(V, V.count), (S, S.count)], inc=[(T, 1)])
                C.emit("vector", lambda e: e.tensor_reduce(
                    out=stats[:, 24:25], in_=ps[1][:, 0:128], axis=AX.X, op=ALU.max),
                    wait=[(T, T.count), (V, V.count)], inc=[(V, 1)])
                C.emit("vector", (lambda t_, mc_: lambda e: e.tensor_scalar(
                    out=stats[:, mc_ + t_:mc_ + t_ + 1], in0=stats[:, 24:25], scalar1=1e-5,
                    scalar2=None, op0=ALU.max))(t, m_col), wait=[(V, V.count)], inc=[(V, 1)])
                C.emit("vector", (lambda t_, mc_: lambda e: e.reciprocal(
                    stats[:, 25:26], stats[:, mc_ + t_:mc_ + t_ + 1]))(t, m_col),
                    wait=[(V, V.count)], inc=[(V, 1)])
                C.emit("vector", lambda e: e.tensor_scalar(
                    out=stats[:, 26:27], in0=stats[:, 25:26], scalar1=127.0, scalar2=None,
                    op0=ALU.mult), wait=[(V, V.count)], inc=[(V, 1)])
                C.emit("tensor", lambda e: e.transpose(
                    ps[3][0:1, 0:128], stats[:, 26:27], ident[:]),
                    wait=[(V, V.count)], inc=[(T, 1)])
                C.emit("vector", (lambda t_: lambda e: e.tensor_copy(
                    aux[0:1, 256 + 128 * t_:256 + 128 * (t_ + 1)], ps[3][0:1, 0:128]))(t),
                    wait=[(T, T.count)], inc=[(V, 1)])

            for t in range(4):
                mrow_chain(6144, 12, t)
            C.emit("tensor", lambda e: e.matmul(
                ps[4][:, 0:TP], ones[0:1, 0:128], aux[0:1, 256:256 + TP],
                start=True, stop=True), wait=[(V, V.count)], inc=[(T, 1)])
            t_bc = T.count
            for c in range(16):
                C.emit("vector", (lambda c_: lambda e: e.tensor_tensor(
                    out=RC[:, TP * c_:TP * (c_ + 1)],
                    in0=RA[:, TP * c_:TP * (c_ + 1)],
                    in1=ps[4][:, 0:TP], op=ALU.mult))(c),
                    wait=[(T, t_bc)] if c == 0 else [], inc=[(V, 1)])
            C.emit("vector", lambda e: e.tensor_scalar(
                out=RB[:, 0:8192], in0=RC[:, 0:8192], scalar1=384.5, scalar2=None,
                op0=ALU.add), inc=[(V, 1)])
            C.emit("vector", lambda e: e.tensor_single_scalar(
                out=RC[:, 0:8192].bitcast(I32), in_=RB[:, 0:8192].bitcast(I32),
                scalar=-32768, op=ALU.bitwise_and), inc=[(V, 1)])
            C.emit("vector", lambda e: e.tensor_scalar(
                out=r1T, in0=RC[:, 0:8192], scalar1=-384.0, scalar2=None,
                op0=ALU.add), inc=[(V, 1)])
            v_r1 = V.count
            tku = []
            vr_og = []
            sq_og = []
            for og in range(4):
                for k in range(16):
                    slot = (k % 2) * 1024
                    src = wupT_scr.rearrange("(q c) o -> c q o", c=16)[k, :, 1024 * og:1024 * (og + 1)]
                    ki = og * 16 + k
                    C.emit("sync", (lambda s_, src_: lambda e: e.dma_start(
                        out=wub[:, s_:s_ + 1024], in_=src_))(slot, src),
                        wait=[(T, tku[ki - 2])] if ki >= 2 else [(V, v_r1), (GP, gp_wdone)], inc=[(LD, 16)])
                    ldu = LD.count
                    for m in range(8):
                        kw = []
                        if m == 0:
                            kw = [(LD, ldu)]
                            if k == 0:
                                kw += [(V, v_r1)] if og == 0 else [(V, vr_og[og - 1])]
                        C.emit("tensor", (lambda k_, m_, s_: lambda e: e.matmul(
                            ps[m_][:, 0:TP], wub[:, s_ + 128 * m_:s_ + 128 * (m_ + 1)],
                            r1T[:, TP * k_:TP * (k_ + 1)],
                            start=(k_ == 0), stop=(k_ == 15)))(k, m, slot),
                            wait=kw,
                            inc=([(T, 1)] if m == 7 else []))
                    tku.append(T.count)
                t_up = T.count
                half = RA if og < 2 else RB
                base = 4096 * (og % 2)
                rcb = 4096 * (og % 2)
                for m in range(8):
                    rw = [(T, t_up)] if m == 0 else []
                    if m == 0 and og >= 2:
                        rw += [(S, sq_og[og - 2])]
                    C.emit("vector", (lambda m_, rb_: lambda e: e.tensor_scalar(
                        out=RC[:, rb_ + TP * m_:rb_ + TP * (m_ + 1)], in0=ps[m_][:, 0:TP],
                        scalar1=0.0, scalar2=None, op0=ALU.max))(m, rcb),
                        wait=rw, inc=[(V, 1)])
                vr_og.append(V.count)
                for m in range(8):
                    C.emit("scalar", (lambda m_, h_, b_, rb_: lambda e: e.square(
                        h_[:, b_ + TP * m_:b_ + TP * (m_ + 1)],
                        RC[:, rb_ + TP * m_:rb_ + TP * (m_ + 1)]))(m, half, base, rcb),
                        wait=[(V, vr_og[-1])] if m == 0 else [], inc=[(S, 1)])
                sq_og.append(S.count)
            s_h1 = S.count
            for hb in range(2):
                for b in range(16):
                    idx = hb * 16 + b
                    half = RA if idx < 16 else RB
                    base = TP * (idx % 16)
                    pb = ps[b % 2]
                    C.emit("tensor", (lambda h_, ba_, pb_: lambda e: e.matmul(
                        pb_[:, 0:TP], badn[:], h_[:, ba_:ba_ + TP],
                        start=True, stop=True))(half, base, pb),
                        wait=[(GP, GP.count), (S, s_h1), (V, V.count)] if b == 0 else [(V, V.count)],
                        inc=[(T, 1)])
                    tb = T.count
                    C.emit("vector", (lambda b_, pb_: lambda e: e.tensor_copy(
                        RC[:, TP * b_:TP * (b_ + 1)], pb_[:, 0:TP]))(b, pb),
                        wait=[(T, tb)], inc=[(V, 1)])
                v_za2 = V.count
                C.emit("gpsimd", (lambda h_: lambda e: e.dma_start(
                    out=zscr[:, 8192 * h_:8192 * (h_ + 1)], in_=RC[:, 0:8192]))(hb),
                    wait=[(V, v_za2), (GP, GP.count)], inc=[(GP, 16)])
            zt = zscr[:, 0:16384].rearrange("(u c) (b t) -> c b u t", u=4, c=32, b=32)
            gp_tile2 = []
            for c in range(32):
                half = RA if c < 16 else RB
                C.emit("gpsimd", (lambda c_, h_: lambda e: e.dma_start(
                    out=h_[:, 512 * (c_ % 16):512 * (c_ % 16 + 1)], in_=zt[c_]))(c, half),
                    wait=[(GP, GP.count)] if c == 0 else [], inc=[(GP, 16)])
                gp_tile2.append(GP.count)
            gp_zb2 = GP.count
            def xh2_dst(c):
                if c < 16:
                    return RC[:, TP * c:TP * (c + 1)]
                if c < 24:
                    return SX[:, TP * (c - 16):TP * (c - 15)]
                return xnf[:, TP * (c - 24):TP * (c - 23)]
            for c in range(32):
                pb = ps[2 + c % 2]
                src_half = RA if c < 16 else RB
                C.emit("tensor", (lambda c_, pb_, sh_: lambda e: e.matmul(
                    pb_[:, 0:TP], h128[:], sh_[:, TP * (c_ % 16):TP * (c_ % 16 + 1)],
                    start=True, stop=True))(c, pb, src_half),
                    wait=[(GP, gp_tile2[c])] + ([(V, V.count)] if c > 0 else []), inc=[(T, 1)])
                tb = T.count
                C.emit("vector", (lambda c_, pb_: lambda e: e.tensor_copy(
                    xh2_dst(c_), pb_[:, 0:TP]))(c, pb),
                    wait=[(T, tb)], inc=[(V, 1)])
            v_xh2 = V.count
            t_sb2 = T.count
            H0, H1, H2, H3 = 6144, 6656, 7168, 7680

            def tree512(src, width, op, dst_off):
                w = width // 2
                C.emit("vector", (lambda w_, s_: lambda e: e.tensor_tensor(
                    out=RB[:, 0:w_] if w_ > 512 else RB[:, dst_off:dst_off + 512],
                    in0=s_[:, 0:w_], in1=s_[:, w_:2 * w_], op=op))(w, src), inc=[(V, 1)])
                cur, nxt = 0, 4096
                while w > 512:
                    w //= 2
                    dsto = nxt if w > 512 else dst_off
                    C.emit("vector", (lambda c_, d_, w_: lambda e: e.tensor_tensor(
                        out=RB[:, d_:d_ + w_], in0=RB[:, c_:c_ + w_], in1=RB[:, c_ + w_:c_ + 2 * w_],
                        op=op))(cur, dsto, w), inc=[(V, 1)])
                    cur, nxt = nxt, cur

            def region_absmax(src, width, dst_off, tmp1, tmp2):
                tree512(src, width, ALU.max, tmp1)
                tree512(src, width, ALU.min, dst_off)
                C.emit("vector", (lambda d_, t_: lambda e: e.tensor_scalar(
                    out=RB[:, t_:t_ + 512], in0=RB[:, d_:d_ + 512], scalar1=-1.0, scalar2=None,
                    op0=ALU.mult))(dst_off, tmp2), inc=[(V, 1)])
                C.emit("vector", (lambda d_, t1_, t2_: lambda e: e.tensor_tensor(
                    out=RB[:, d_:d_ + 512], in0=RB[:, t1_:t1_ + 512], in1=RB[:, t2_:t2_ + 512],
                    op=ALU.max))(dst_off, tmp1, tmp2), inc=[(V, 1)])

            C.emit("vector", None, wait=[(V, v_xh2), (T, t_sb2)])
            region_absmax(RC[:, 0:8192], 8192, H0, H1, H2)
            region_absmax(SX[:, 0:4096], 4096, H1, H2, H3)
            C.emit("vector", lambda e: e.tensor_tensor(
                out=RB[:, H2:H2 + 512], in0=RB[:, H0:H0 + 512], in1=RB[:, H1:H1 + 512],
                op=ALU.max), inc=[(V, 1)])
            region_absmax(xnf[:, 0:4096], 4096, H0, H1, H3)
            C.emit("vector", lambda e: e.tensor_tensor(
                out=RB[:, H3:H3 + 512], in0=RB[:, H2:H2 + 512], in1=RB[:, H0:H0 + 512],
                op=ALU.max), inc=[(V, 1)])
            for t in range(4):
                mrow_chain(H3, 16, t)
            C.emit("tensor", lambda e: e.matmul(
                ps[4][:, 0:TP], ones[0:1, 0:128], aux[0:1, 256:256 + TP],
                start=True, stop=True), wait=[(V, V.count)], inc=[(T, 1)])
            t_bc2 = T.count
            C.emit("vector", lambda e: e.tensor_copy(aux[:, 0:TP], ps[4][:, 0:TP]),
                   wait=[(T, t_bc2)], inc=[(V, 1)])
            r2T = r_bf[:, 0:16384]

            def scale_region(src, n_tiles, bf_off, first):
                for c in range(n_tiles):
                    C.emit("vector", (lambda c_, s_: lambda e: e.tensor_tensor(
                        out=RB[:, TP * c_:TP * (c_ + 1)],
                        in0=s_[:, TP * c_:TP * (c_ + 1)],
                        in1=aux[:, 0:TP], op=ALU.mult))(c, src),
                        inc=[(V, 1)])
                w = TP * n_tiles
                C.emit("vector", (lambda w_: lambda e: e.tensor_scalar(
                    out=RC[:, 0:w_], in0=RB[:, 0:w_], scalar1=384.5, scalar2=None,
                    op0=ALU.add))(w), inc=[(V, 1)])
                C.emit("vector", (lambda w_: lambda e: e.tensor_single_scalar(
                    out=RB[:, 0:w_].bitcast(I32), in_=RC[:, 0:w_].bitcast(I32),
                    scalar=-32768, op=ALU.bitwise_and))(w), inc=[(V, 1)])
                C.emit("vector", (lambda w_, bo_: lambda e: e.tensor_scalar(
                    out=r2T[:, bo_:bo_ + w_], in0=RB[:, 0:w_], scalar1=-384.0, scalar2=None,
                    op0=ALU.add))(w, bf_off), inc=[(V, 1)])

            scale_region(RC[:, 0:8192], 16, 0, True)
            v_sA = V.count
            scale_region(SX[:, 0:4096], 8, 8192, False)
            v_sB = V.count
            scale_region(xnf[:, 0:4096], 8, 12288, False)
            v_sC = V.count
            v_r2 = V.count
            for t in range(4):
                C.emit("vector", (lambda t_: lambda e: e.tensor_scalar(
                    out=stats[:, 24:25], in0=stats[:, 12 + t_:13 + t_], scalar1=stats[:, 8:9],
                    scalar2=1.0 / 127.0, op0=ALU.mult, op1=ALU.mult))(t),
                    wait=[(V, V.count)], inc=[(V, 1)])
                C.emit("scalar", lambda e: e.square(stats[:, 26:27], stats[:, 24:25]),
                       wait=[(V, V.count)], inc=[(S, 1)])
                C.emit("vector", (lambda t_: lambda e: e.tensor_scalar(
                    out=stats[:, 25:26], in0=stats[:, 16 + t_:17 + t_], scalar1=stats[:, 10:11],
                    scalar2=1.0 / 127.0, op0=ALU.mult, op1=ALU.mult))(t),
                    wait=[(S, S.count), (V, V.count)], inc=[(V, 1)])
                C.emit("vector", (lambda t_: lambda e: e.tensor_tensor(
                    out=stats[:, 20 + t_:21 + t_], in0=stats[:, 26:27], in1=stats[:, 25:26],
                    op=ALU.mult))(t), wait=[(V, V.count)], inc=[(V, 1)])
            v_f = V.count
            for p2 in range(2):
                tkd = []
                for k in range(32):
                    slot = (k % 2) * 1024
                    src = wdnT_scr.rearrange("(q c) o -> c q o", c=32)[k, :, 1024 * p2:1024 * (p2 + 1)]
                    C.emit("sync", (lambda s_, src_: lambda e: e.dma_start(
                        out=wdb[:, s_:s_ + 1024], in_=src_))(slot, src),
                        wait=[(T, tkd[k - 2])] if k >= 2 else ([(V, v_sA), (GP, gp_wdone)] if p2 == 0 else [(V, V.count)]),
                        inc=[(LD, 16)])
                    ldd = LD.count
                    for mc in range(4):
                        for n in range(2):
                            gate = []
                            if mc == 0 and n == 0:
                                gate = [(LD, ldd)]
                                if k == 0:
                                    gate += [(V, v_sA)] if p2 == 0 else [(V, V.count)]
                                elif k == 16 and p2 == 0:
                                    gate += [(V, v_sB)]
                                elif k == 24 and p2 == 0:
                                    gate += [(V, v_sC)]
                            C.emit("tensor", (lambda k_, mc_, n_, s_: lambda e: e.matmul(
                                ps[mc_ * 2 + n_][:, 0:TP], r2T[:, TP * k_ + 128 * mc_:TP * k_ + 128 * (mc_ + 1)],
                                wdb[:, s_ + 512 * n_:s_ + 512 * (n_ + 1)],
                                start=(k_ == 0), stop=(k_ == 31)))(k, mc, n, slot),
                                wait=gate,
                                inc=([(T, 1)] if (mc == 3 and n == 1) else []))
                    tkd.append(T.count)
                t_dn = T.count
                for mc in range(4):
                    for n in range(2):
                        C.emit("vector", (lambda mc_, n_, p_: lambda e: e.tensor_scalar(
                            out=RC[:, 2048 * mc_ + 1024 * p_ + 512 * n_:2048 * mc_ + 1024 * p_ + 512 * (n_ + 1)],
                            in0=ps[mc_ * 2 + n_][:, 0:TP], scalar1=stats[:, 20 + mc_:21 + mc_],
                            scalar2=None, op0=ALU.mult))(mc, n, p2),
                            wait=[(T, t_dn), (V, V.count)] if (mc == 0 and n == 0) else [], inc=[(V, 1)])
            t_dn_all = T.count
            v_dn = V.count
            for mc in range(4):
                dst = SX[:, 2048 * mc:2048 * (mc + 1)] if mc < 2 else xnf[:, 2048 * (mc - 2):2048 * (mc - 1)]
                C.emit("scalar", (lambda mc_, d_: lambda e: e.activation(
                    out=d_, in_=RC[:, 2048 * mc_:2048 * (mc_ + 1)], func=ACT.Silu))(mc, dst),
                    wait=[(V, v_dn)] if mc == 0 else [], inc=[(S, 1)])
            s_g = S.count
            for i in range(4):
                xrow = HALO + TP * pair + 128 * i
                C.emit("sync", (lambda i_, r0: lambda e: e.dma_start(
                    out=RB[:, 2048 * i_:2048 * (i_ + 1)], in_=x_in[r0:r0 + 128, :]))(i, xrow),
                    wait=[(T, t_dn_all)] if i == 0 else [], inc=[(LD, 16)])
            ld_x2 = LD.count
            C.emit("gpsimd", (lambda p_: lambda e: e.dma_start(
                out=RA[:, 0:8192], in_=s_scr[:, 8192 * p_:8192 * (p_ + 1)]))(pair),
                wait=[(T, t_dn_all), (GP, GP.count)], inc=[(GP, 16)])
            gp_s = GP.count
            for i in range(4):
                C.emit("vector", (lambda i_: lambda e: e.tensor_tensor(
                    out=RC[:, 2048 * i_:2048 * (i_ + 1)],
                    in0=SX[:, 2048 * i_:2048 * (i_ + 1)] if i_ < 2 else xnf[:, 2048 * (i_ - 2):2048 * (i_ - 1)],
                    in1=RA[:, 2048 * i_:2048 * (i_ + 1)], op=ALU.mult))(i),
                    wait=[(S, s_g), (LD, ld_x2), (GP, gp_s)] if i == 0 else [], inc=[(V, 1)])
                C.emit("vector", (lambda i_: lambda e: e.tensor_tensor(
                    out=RA[:, 2048 * i_:2048 * (i_ + 1)], in0=RB[:, 2048 * i_:2048 * (i_ + 1)],
                    in1=RC[:, 2048 * i_:2048 * (i_ + 1)], op=ALU.add))(i),
                    inc=[(V, 1)])
                vres = V.count
                row0 = TP * pair + 128 * i
                C.emit("gpsimd", (lambda i_, r0: lambda e: e.dma_start(
                    out=out_hbm[r0:r0 + 128, :], in_=RA[:, 2048 * i_:2048 * (i_ + 1)]))(i, row0),
                    wait=[(V, vres)], inc=[(GP, 16)])

        for _pair in range(2):
            emit_attn_half(2 * _pair)
            emit_attn_half(2 * _pair + 1)
            emit_ffn_pair(_pair)

        C.emit("gpsimd", None, wait=[(GP, GP.count)])

        with nc.Block() as block:
            @block.sync
            def _(e):
                C.replay("sync", e)

            @block.gpsimd
            def _(e):
                C.replay("gpsimd", e)

            @block.tensor
            def _(e):
                C.replay("tensor", e)

            @block.vector
            def _(e):
                C.replay("vector", e)

            @block.scalar
            def _(e):
                C.replay("scalar", e)

    return nc


_NC_CACHE = None


def kernel(L0_attn_x, L0_attn_Wq1, L0_attn_Wq2, L0_attn_Wk1, L0_attn_Wk2,
           L0_attn_Wv, L0_attn_Wo, L0_attn_norm_w, L0_ffn_w_up, L0_ffn_w_down):
    global _NC_CACHE
    x = np.ascontiguousarray(np.asarray(L0_attn_x)[0], dtype=np.float32)
    H16, H32, H128 = _hadamard(16), _hadamard(32), _hadamard(128)
    ba_up = (np.kron(np.eye(8, dtype=np.float32), H16) * np.float32(1.0 / np.sqrt(2048.0))).astype(np.float32)
    ba_dn = (np.kron(np.eye(4, dtype=np.float32), H32) * np.float32(2.0 ** -6)).astype(np.float32)
    ident = np.eye(128, dtype=np.float32)
    common = {
        "wq1": np.ascontiguousarray(L0_attn_Wq1, np.float32),
        "wq2": np.ascontiguousarray(L0_attn_Wq2, np.float32),
        "wk1": np.ascontiguousarray(L0_attn_Wk1, np.float32),
        "wk2": np.ascontiguousarray(L0_attn_Wk2, np.float32),
        "wv": np.ascontiguousarray(L0_attn_Wv, np.float32),
        "wo": np.ascontiguousarray(L0_attn_Wo, np.float32),
        "norm_w": np.ascontiguousarray(np.asarray(L0_attn_norm_w, np.float32).reshape(32, 128).T),
        "w_up": np.ascontiguousarray(L0_ffn_w_up, np.float32),
        "w_down": np.ascontiguousarray(L0_ffn_w_down, np.float32),
        "ident": ident, "ba_up": ba_up, "ba_dn": ba_dn, "h128": H128,
    }
    in_maps = []
    for c in range(8):
        lo = S_CORE * c
        halo = np.zeros((HALO, E), np.float32) if c == 0 else x[lo - HALO:lo]
        x_pad = np.ascontiguousarray(np.concatenate([halo, x[lo:lo + S_CORE]], 0))
        in_maps.append({"x_pad": x_pad, **common})
    if _NC_CACHE is None:
        _NC_CACHE = build_nc()
    res = run_bass_kernel_spmd(_NC_CACHE, in_maps, core_ids=list(range(8)))
    out = np.concatenate([res.results[c]["out"] for c in range(8)], 0)
    return out[None].astype(np.float32)


# version 23 (final)
# speedup vs baseline: 1.0468x; 1.0260x over previous
import numpy as np
import concourse.bass as bass
import concourse.mybir as mybir
from concourse.bass_utils import run_bass_kernel_spmd

F32 = mybir.dt.float32
I32 = mybir.dt.int32
BF16 = mybir.dt.bfloat16
ALU = mybir.AluOpType
ACT = mybir.ActivationFunctionType
AX = mybir.AxisListType

E = 2048
E2 = 4096
HD = 128
TO = 256
HALO = 128
NSC = 4
S_CORE = NSC * TO
S_PAD = S_CORE + HALO


def _hadamard(n):
    H = np.array([[1.0]], dtype=np.float32)
    while H.shape[0] < n:
        H = np.block([[H, H], [H, -H]]).astype(np.float32)
    return H


class _Sem:
    __slots__ = ("name", "count", "h")

    def __init__(self, name):
        self.name = name
        self.count = 0
        self.h = None


class _Ctx:

    def __init__(self, nc):
        self.nc = nc
        self.streams = {k: [] for k in ("sync", "gpsimd", "tensor", "vector", "scalar")}
        self.sems = {k: _Sem(k) for k in ("LD", "GP", "T", "V", "S", "L2")}

    def emit(self, eng, fn, wait=(), inc=()):
        wait = [(s, v) for (s, v) in wait if v > 0]
        self.streams[eng].append((wait, fn, list(inc)))
        for s, n in inc:
            s.count += n

    def barrier(self):
        sems = list(self.sems.values())
        waits = [(x, x.count) for x in sems]
        for eng in self.streams:
            self.emit(eng, None, wait=list(waits))

    def replay(self, eng_name, e):
        for wait, fn, inc in self.streams[eng_name]:
            for s, v in wait:
                e.wait_ge(s.h, v)
            if fn is None:
                continue
            inst = fn(e)
            for s, n in inc:
                inst.then_inc(s.h, n)


def build_nc():
    nc = bass.Bass(trn_type="TRN2", target_bir_lowering=False, detect_race_conditions=False)

    x_in = nc.dram_tensor("x_pad", [S_PAD, E], F32, kind="ExternalInput").ap()
    wq1 = nc.dram_tensor("wq1", [E, HD], F32, kind="ExternalInput").ap()
    wq2 = nc.dram_tensor("wq2", [HD, E2], F32, kind="ExternalInput").ap()
    wk1 = nc.dram_tensor("wk1", [E, HD], F32, kind="ExternalInput").ap()
    wk2 = nc.dram_tensor("wk2", [HD, E2], F32, kind="ExternalInput").ap()
    wv = nc.dram_tensor("wv", [E, E2], F32, kind="ExternalInput").ap()
    wo = nc.dram_tensor("wo", [E2, E], F32, kind="ExternalInput").ap()
    nw_in = nc.dram_tensor("norm_w", [128, 32], F32, kind="ExternalInput").ap()
    wup_in = nc.dram_tensor("w_up", [E2, E], F32, kind="ExternalInput").ap()
    wdn_in = nc.dram_tensor("w_down", [E, E2], F32, kind="ExternalInput").ap()
    ident_in = nc.dram_tensor("ident", [128, 128], F32, kind="ExternalInput").ap()
    baup_in = nc.dram_tensor("ba_up", [128, 128], F32, kind="ExternalInput").ap()
    badn_in = nc.dram_tensor("ba_dn", [128, 128], F32, kind="ExternalInput").ap()
    h128_in = nc.dram_tensor("h128", [128, 128], F32, kind="ExternalInput").ap()
    out_hbm = nc.dram_tensor("out", [S_CORE, E], F32, kind="ExternalOutput").ap()

    zscr = nc.dram_tensor("zscr", [128, 16384], F32).ap()
    s_scr = nc.dram_tensor("s_scr", [128, 16384], F32).ap()
    wupT_scr = nc.dram_tensor("wupT_scr", [E, E2], BF16).ap()
    wdnT_scr = nc.dram_tensor("wdnT_scr", [E2, E], BF16).ap()

    from contextlib import ExitStack
    with ExitStack() as st:
        def sb(name, shape, dt=F32):
            return st.enter_context(nc.sbuf_tensor(name, shape, dt))

        wq1_sb = sb("wq1_sb", [128, E])
        wk1_sb = sb("wk1_sb", [128, E])
        nw_sb = sb("nw_sb", [128, 32])
        state = sb("state", [128, 32])
        ident = sb("ident_sb", [128, 128])
        baup = sb("baup_sb", [128, 128])
        badn = sb("badn_sb", [128, 128])
        h128 = sb("h128_sb", [128, 128])
        ones = sb("ones_sb", [128, 128])
        ident_bf = sb("identbf_sb", [128, 128], BF16)
        q1T = sb("q1T", [128, TO])
        lf1T = sb("lf1T", [128, 384])
        stats = sb("stats", [128, 32])
        aux = sb("aux", [128, 768])
        rmsacc = sb("rmsacc", [128, 2 * TO])
        RA = sb("RA", [128, 8192])
        RB = sb("RB", [128, 8192])
        RC = sb("RC", [128, 8192])
        SX = sb("SX", [128, 2 * E])
        XN = sb("XN", [128, 16 * 384])
        wvb = sb("wvb", [128, 3072])
        wqg = sb("wqg", [128, 1024])
        wkg = sb("wkg", [128, 1024])
        wob = sb("wob", [128, 4096])
        wub = sb("wub", [128, 2048], BF16)
        wdb = sb("wdb", [128, 2048], BF16)

        ps = [st.enter_context(nc.psum_tensor(f"ps{i}", [128, 512], F32)) for i in range(8)]

        C = _Ctx(nc)
        for s in C.sems.values():
            s.h = st.enter_context(nc.semaphore(s.name))
        LD, GP, T, V, S, L2 = (C.sems[k] for k in ("LD", "GP", "T", "V", "S", "L2"))

        xn_bf = XN[:].bitcast(BF16)

        C.emit("sync", lambda e: e.dma_start(out=ident[:], in_=ident_in), inc=[(LD, 16)])
        C.emit("sync", lambda e: e.dma_start(out=baup[:], in_=baup_in), inc=[(LD, 16)])
        C.emit("sync", lambda e: e.dma_start(out=badn[:], in_=badn_in), inc=[(LD, 16)])
        C.emit("sync", lambda e: e.dma_start(out=h128[:], in_=h128_in), inc=[(LD, 16)])
        C.emit("sync", lambda e: e.dma_start(out=nw_sb[:, 0:32], in_=nw_in), inc=[(LD, 16)])
        C.emit("sync", lambda e: e.dma_start(
            out=wq1_sb[:].rearrange("p (k m) -> p k m", k=16),
            in_=wq1.rearrange("(k p) m -> p k m", p=128)), inc=[(LD, 16)])
        C.emit("sync", lambda e: e.dma_start(
            out=wk1_sb[:].rearrange("p (k m) -> p k m", k=16),
            in_=wk1.rearrange("(k p) m -> p k m", p=128)), inc=[(LD, 16)])
        C.emit("vector", lambda e: e.memset(state[:], 0.0), inc=[(V, 1)])
        C.emit("vector", lambda e: e.memset(ones[:], 1.0), inc=[(V, 1)])

        wacc = aux
        red_u, red_d, ld_u, ld_d = [], [], [], []
        for i in range(32):
            slot = (i % 2) * 4096
            src = wup_in[128 * i:128 * (i + 1), :]
            C.emit("sync", (lambda s_, src_: lambda e: e.dma_start(out=RA[:, s_:s_ + 2048], in_=src_))(slot, src),
                   wait=[(V, red_u[i - 2])] if i >= 2 else [], inc=[(LD, 16)])
            ld_u.append(LD.count)
            C.emit("vector", (lambda s_, i_: lambda e: e.tensor_reduce(
                out=wacc[:, i_:i_ + 1], in_=RA[:, s_:s_ + 2048], axis=AX.X, op=ALU.add,
                apply_absolute_value=True))(slot, i),
                wait=[(LD, ld_u[-1])], inc=[(V, 1)])
            red_u.append(V.count)
            if i % 2 == 1 and i // 2 < 16:
                j = i // 2
                slot2 = (j % 2) * 4096
                src2 = wdn_in[128 * j:128 * (j + 1), :]
                C.emit("scalar", (lambda s_, src_: lambda e: e.dma_start(out=RB[:, s_:s_ + 4096], in_=src_))(slot2, src2),
                       wait=[(V, red_d[j - 2])] if j >= 2 else [], inc=[(L2, 16)])
                ld_d.append(L2.count)
                C.emit("vector", (lambda s_, j_: lambda e: e.tensor_reduce(
                    out=wacc[:, 32 + j_:33 + j_], in_=RB[:, s_:s_ + 4096], axis=AX.X, op=ALU.add,
                    apply_absolute_value=True))(slot2, j),
                    wait=[(L2, ld_d[-1])], inc=[(V, 1)])
                red_d.append(V.count)
        C.emit("vector", lambda e: e.tensor_reduce(out=stats[:, 21:22], in_=wacc[:, 0:32],
                                                   axis=AX.X, op=ALU.add), inc=[(V, 1)])
        C.emit("vector", lambda e: e.tensor_reduce(out=stats[:, 22:23], in_=wacc[:, 32:48],
                                                   axis=AX.X, op=ALU.add), inc=[(V, 1)])
        vham = V.count
        C.emit("tensor", lambda e: e.matmul(ps[0][0:1, 0:2], ones[:, 0:1], stats[:, 21:23],
                                            start=True, stop=True),
               wait=[(V, vham)], inc=[(T, 1)])
        C.emit("vector", lambda e: e.tensor_copy(aux[0:1, 0:2], ps[0][0:1, 0:2]),
               wait=[(T, T.count)], inc=[(V, 1)])
        C.emit("tensor", lambda e: e.matmul(ps[1][:, 0:2], ones[0:1, 0:128], aux[0:1, 0:2],
                                            start=True, stop=True),
               wait=[(V, V.count)], inc=[(T, 1)])
        C.emit("vector", lambda e: e.tensor_scalar(out=stats[:, 8:9], in0=ps[1][:, 0:1],
                                                   scalar1=1.0 / (4096.0 * 2048.0), scalar2=1e-5,
                                                   op0=ALU.mult, op1=ALU.max),
               wait=[(T, T.count)], inc=[(V, 1)])
        C.emit("vector", lambda e: e.tensor_scalar(out=stats[:, 10:11], in0=ps[1][:, 1:2],
                                                   scalar1=1.0 / (2048.0 * 4096.0), scalar2=1e-5,
                                                   op0=ALU.mult, op1=ALU.max), inc=[(V, 1)])
        C.emit("vector", lambda e: e.reciprocal(stats[:, 9:10], stats[:, 8:9]), inc=[(V, 1)])
        C.emit("vector", lambda e: e.reciprocal(stats[:, 11:12], stats[:, 10:11]), inc=[(V, 1)])
        vscales = V.count

        ra_bf = RA[:].bitcast(BF16)
        C.emit("scalar", lambda e: e.copy(out=ident_bf[:], in_=ident[:]),
               wait=[(LD, 16)], inc=[(S, 1)])
        C.barrier()

        blk_hist = []
        tp_done = []
        tp_grp = []
        stg = RC[:].bitcast(BF16)

        def w_quant_pass(win, nblk, ncols, rws_col, scrT, ktiles):
            for i in range(nblk):
                g = len(blk_hist)
                slot = (i % 2) * 4096
                src = win[128 * i:128 * (i + 1), :]
                dw = []
                if g >= 2:
                    h = blk_hist[g - 2]
                    dw = [(V, h["cmp"]), (T, h["tdone"]), (GP, h["gp"])]
                C.emit("sync", (lambda s_, src_, nc_: lambda e: e.dma_start(
                    out=RA[:, s_:s_ + nc_], in_=src_))(slot, src, ncols),
                    wait=dw, inc=[(LD, 16)])
                ldw = LD.count
                C.emit("vector", (lambda s_, nc_: lambda e: e.tensor_scalar(
                    out=RB[:, s_:s_ + nc_], in0=RA[:, s_:s_ + nc_], scalar1=stats[:, rws_col:rws_col + 1],
                    scalar2=0.5, op0=ALU.mult, op1=ALU.is_gt))(slot, ncols),
                    wait=[(LD, ldw), (V, max(0, vscales))], inc=[(V, 1)])
                C.emit("vector", (lambda s_, nc_: lambda e: e.tensor_scalar(
                    out=RC[:, s_:s_ + nc_], in0=RA[:, s_:s_ + nc_], scalar1=stats[:, rws_col:rws_col + 1],
                    scalar2=-0.5, op0=ALU.mult, op1=ALU.is_lt))(slot, ncols), inc=[(V, 1)])
                C.emit("vector", (lambda s_, nc_: lambda e: e.tensor_tensor(
                    out=ra_bf[:, 2 * s_:2 * s_ + nc_], in0=RB[:, s_:s_ + nc_], in1=RC[:, s_:s_ + nc_],
                    op=ALU.subtract))(slot, ncols), inc=[(V, 1)])
                vt = V.count
                stg_off = (i % 2) * 8192
                for k in range(ktiles):
                    pk = len(tp_done)
                    grp = pk // 4
                    pb = ps[grp % 8]
                    off = 64 * (pk % 4)
                    C.emit("tensor", (lambda s_, k_, pb_, o_: lambda e: e.transpose(
                        pb_[:, o_:o_ + 64].bitcast(BF16),
                        ra_bf[:, 2 * s_ + 128 * k_:2 * s_ + 128 * (k_ + 1)],
                        ident_bf[:]))(slot, k, pb, off),
                        wait=([(V, vt)] if k == 0 else []) +
                             ([(S, tp_grp[grp - 8])] if (pk % 4 == 0 and grp >= 8) else []),
                        inc=[(T, 1)])
                    tp_done.append(None)
                    if pk % 4 == 3:
                        tc = T.count
                        C.emit("scalar", (lambda so_, k_, pb_: lambda e: e.copy(
                            out=stg[:, so_ + 128 * (k_ - 3):so_ + 128 * (k_ + 1)],
                            in_=pb_[:, 0:256].bitcast(BF16)))(stg_off, k, pb),
                            wait=[(T, tc)] + ([(GP, blk_hist[g - 2]["gp"])] if (k == 3 and g >= 2) else []),
                            inc=[(S, 1)])
                        tp_grp.append(S.count)
                sc = S.count
                dst = scrT.rearrange("(k p) o -> p k o", p=128)[:, :, 128 * i:128 * (i + 1)]
                C.emit("gpsimd", (lambda so_, dst_, k_: lambda e: e.dma_start(
                    out=dst_, in_=stg[:, so_:so_ + 128 * k_].rearrange("p (k o) -> p k o", k=k_)))(stg_off, dst, ktiles),
                    wait=[(S, sc)], inc=[(GP, 16)])
                blk_hist.append({"cmp": vt, "tdone": T.count, "gp": GP.count, "sdone": sc})

        w_quant_pass(wup_in, 32, 2048, 9, wupT_scr, 16)
        w_quant_pass(wdn_in, 16, 4096, 11, wdnT_scr, 32)
        gp_wdone = GP.count

        C.barrier()
        def emit_attn_half(sc):
            ts = 384 if sc == 0 else 256
            ofs = 0 if sc == 0 else 384 + 256 * (sc - 1)
            nt = ts // 128
            oofs = 128 if sc == 0 else 0

            C.barrier()
            for i in range(nt):
                owned_i = i - (1 if sc == 0 else 0)
                xst = 4096
                C.emit("sync", (lambda r0: lambda e: e.dma_start(
                    out=RB[:, xst:xst + 2048], in_=x_in[r0:r0 + 128, :]))(ofs + 128 * i),
                    wait=[(V, V.count), (T, T.count), (GP, GP.count)], inc=[(LD, 16)])
                ldx = LD.count
                C.emit("vector", lambda e: e.tensor_reduce(
                    out=stats[:, 5:6], in_=RB[:, xst:xst + 2048], axis=AX.X, op=ALU.add),
                    wait=[(LD, ldx)], inc=[(V, 1)])
                C.emit("scalar", lambda e: e.activation(
                    out=RC[:, 0:2048], in_=RB[:, xst:xst + 2048], func=ACT.Square,
                    accum_out=stats[:, 6:7]),
                    wait=[(LD, ldx)], inc=[(S, 1)])
                C.emit("vector", lambda e: e.tensor_scalar(
                    out=stats[:, 0:1], in0=stats[:, 5:6], scalar1=1.0 / 2048.0, scalar2=None,
                    op0=ALU.mult), wait=[(S, S.count), (V, V.count)], inc=[(V, 1)])
                C.emit("vector", lambda e: e.tensor_scalar(
                    out=stats[:, 1:2], in0=stats[:, 6:7], scalar1=1.0 / 2048.0, scalar2=1e-5,
                    op0=ALU.mult, op1=ALU.add), inc=[(V, 1)])
                C.emit("scalar", lambda e: e.square(stats[:, 2:3], stats[:, 0:1]),
                       wait=[(V, V.count)], inc=[(S, 1)])
                C.emit("vector", lambda e: e.tensor_tensor(
                    out=stats[:, 7:8], in0=stats[:, 1:2], in1=stats[:, 2:3], op=ALU.subtract),
                    wait=[(S, S.count), (V, V.count)], inc=[(V, 1)])
                C.emit("scalar", lambda e: e.sqrt(stats[:, 3:4], stats[:, 7:8]),
                       wait=[(V, V.count)], inc=[(S, 1)])
                C.emit("vector", lambda e: e.reciprocal(stats[:, 4:5], stats[:, 3:4]),
                       wait=[(S, S.count)], inc=[(V, 1)])
                C.emit("vector", lambda e: e.tensor_scalar(
                    out=RC[:, 4096:6144], in0=RB[:, xst:xst + 2048], scalar1=stats[:, 0:1],
                    scalar2=stats[:, 4:5], op0=ALU.subtract, op1=ALU.mult),
                    wait=[(T, T.count), (V, V.count)], inc=[(V, 1)])
                vxn = V.count
                for k in range(16):
                    pb = ps[k % 2]
                    C.emit("tensor", (lambda k_, pb_: lambda e: e.transpose(
                        pb_[:, 0:128], RC[:, 4096 + 128 * k_:4096 + 128 * (k_ + 1)], ident[:]))(k, pb),
                        wait=[(V, vxn), (S, S.count)], inc=[(T, 1)])
                    tc = T.count
                    C.emit("scalar", (lambda k_, i_, pb_: lambda e: e.copy(
                        out=XN[:, 384 * k_ + 128 * i_:384 * k_ + 128 * (i_ + 1)], in_=pb_[:, 0:128]))(k, i, pb),
                        wait=[(T, tc)], inc=[(S, 1)])
            s_xn = S.count
            for (w1sb, dstT, n0, nlen) in ((wq1_sb, q1T, oofs, TO), (wk1_sb, lf1T, 0, ts)):
                for k in range(16):
                    C.emit("tensor", (lambda k_, w_, n0_, nl_: lambda e: e.matmul(
                        ps[2][:, 0:nl_], w_[:, 128 * k_:128 * (k_ + 1)],
                        XN[:, 384 * k_ + n0_:384 * k_ + n0_ + nl_],
                        start=(k_ == 0), stop=(k_ == 15)))(k, w1sb, n0, nlen),
                        wait=([(S, s_xn), (V, V.count)] if k == 0 else []),
                        inc=([(T, 1)] if k == 15 else []))
                tq = T.count
                C.emit("vector", (lambda d_, nl_: lambda e: e.tensor_copy(d_[:, 0:nl_], ps[2][:, 0:nl_]))(dstT, nlen),
                       wait=[(T, tq)], inc=[(V, 1)])
            v_q1 = V.count
            for g in range(4):
                gw = [(V, v_q1)] if g == 0 else []
                C.emit("sync", (lambda g_: lambda e: e.dma_start(
                    out=wqg[:], in_=wq2[:, 1024 * g_:1024 * (g_ + 1)]))(g),
                    wait=gw + [(T, T.count)], inc=[(LD, 16)])
                C.emit("sync", (lambda g_: lambda e: e.dma_start(
                    out=wkg[:], in_=wk2[:, 1024 * g_:1024 * (g_ + 1)]))(g), inc=[(LD, 16)])
                ld_qk = LD.count
                for m in range(8):
                    C.emit("tensor", (lambda m_: lambda e: e.matmul(
                        ps[m_][:, 0:ts], wkg[:, 128 * m_:128 * (m_ + 1)], lf1T[:, 0:ts],
                        start=True, stop=True))(m),
                        wait=([(LD, ld_qk), (S, S.count), (V, V.count)] if m == 0 else []),
                        inc=[(T, 1)])
                    tlf = T.count
                    C.emit("scalar", (lambda m_, ts_: lambda e: e.activation(
                        out=RC[:, 4096 + ts_ * m_:4096 + ts_ * (m_ + 1)], in_=ps[m_][:, 0:ts_],
                        func=ACT.Sigmoid))(m, ts),
                        wait=[(T, tlf)] + ([(V, V.count)] if m == 0 else []), inc=[(S, 1)])
                    C.emit("scalar", (lambda m_, ts_: lambda e: e.activation(
                        out=RA[:, ts_ * m_:ts_ * (m_ + 1)], in_=RC[:, 4096 + ts_ * m_:4096 + ts_ * (m_ + 1)],
                        func=ACT.Sigmoid))(m, ts),
                        wait=[(GP, GP.count)] if (g == 0 and m == 0) else [], inc=[(S, 1)])
                s_a = S.count
                C.emit("vector", lambda e: e.tensor_scalar(
                    out=RC[:, 4096:4096 + 8 * ts], in0=RA[:, 0:8 * ts], scalar1=-1.0, scalar2=1.0,
                    op0=ALU.mult, op1=ALU.add), wait=[(S, s_a)], inc=[(V, 1)])
                v_k = V.count
                tk = []
                for k in range(16):
                    slot = (k % 3) * 1024
                    C.emit("sync", (lambda g_, k_, s_: lambda e: e.dma_start(
                        out=wvb[:, s_:s_ + 1024], in_=wv[128 * k_:128 * (k_ + 1), 1024 * g_:1024 * (g_ + 1)]))(g, k, slot),
                        wait=[(T, tk[k - 3])] if k >= 3 else ([(S, s_a)] if k == 0 else []), inc=[(LD, 16)])
                    ldv = LD.count
                    for m in range(8):
                        C.emit("tensor", (lambda k_, m_, s_: lambda e: e.matmul(
                            ps[m_][:, 0:ts], wvb[:, s_ + 128 * m_:s_ + 128 * (m_ + 1)],
                            XN[:, 384 * k_:384 * k_ + ts],
                            start=(k_ == 0), stop=(k_ == 15)))(k, m, slot),
                            wait=([(LD, ldv)] if m == 0 else []) + ([(V, v_k)] if k == 0 and m == 0 else []),
                            inc=([(T, 1)] if m == 7 else []))
                    tk.append(T.count)
                t_v = T.count
                for m in range(8):
                    C.emit("vector", (lambda m_: lambda e: e.tensor_tensor(
                        out=RB[:, ts * m_:ts * (m_ + 1)], in0=RC[:, 4096 + ts * m_:4096 + ts * (m_ + 1)],
                        in1=ps[m_][:, 0:ts], op=ALU.mult))(m),
                        wait=[(T, t_v)] if m == 0 else [], inc=[(V, 1)])
                    C.emit("vector", (lambda g_, m_: lambda e: e.tensor_tensor_scan(
                        out=RC[:, ts * m_:ts * (m_ + 1)], data0=RA[:, ts * m_:ts * (m_ + 1)],
                        data1=RB[:, ts * m_:ts * (m_ + 1)], initial=state[:, 8 * g_ + m_:8 * g_ + m_ + 1],
                        op0=ALU.mult, op1=ALU.add))(g, m), inc=[(V, 1)])
                    C.emit("vector", (lambda g_, m_: lambda e: e.tensor_copy(
                        state[:, 8 * g_ + m_:8 * g_ + m_ + 1], RC[:, ts * m_ + ts - 1:ts * m_ + ts]))(g, m),
                        wait=[(V, V.count)], inc=[(V, 1)])
                v_scan = V.count
                for m in range(8):
                    C.emit("tensor", (lambda m_: lambda e: e.matmul(
                        ps[m_][:, 0:TO], wqg[:, 128 * m_:128 * (m_ + 1)], q1T[:, 0:TO],
                        start=True, stop=True))(m),
                        wait=([(V, v_scan)] if m == 0 else []), inc=[(T, 1)])
                    tqm = T.count
                    C.emit("scalar", (lambda m_: lambda e: e.activation(
                        out=RA[:, TO * m_:TO * (m_ + 1)], in_=ps[m_][:, 0:TO], func=ACT.Silu))(m),
                        wait=[(T, tqm)], inc=[(S, 1)])
                s_q = S.count
                for m in range(8):
                    C.emit("vector", (lambda m_: lambda e: e.tensor_tensor(
                        out=RB[:, TO * m_:TO * (m_ + 1)], in0=RA[:, TO * m_:TO * (m_ + 1)],
                        in1=RC[:, ts * m_ + oofs:ts * m_ + oofs + TO], op=ALU.mult))(m),
                        wait=[(S, s_q)] if m == 0 else [], inc=[(V, 1)])
                v_out = V.count
                for m in range(8):
                    C.emit("scalar", (lambda m_: lambda e: e.square(
                        RC[:, 4096 + TO * m_:4096 + TO * (m_ + 1)], RB[:, TO * m_:TO * (m_ + 1)]))(m),
                        wait=[(V, v_out)] if m == 0 else [], inc=[(S, 1)])
                s_sq = S.count
                SL = 6144
                for j in range(4):
                    C.emit("vector", (lambda j_: lambda e: e.tensor_tensor(
                        out=RC[:, SL + TO * j_:SL + TO * (j_ + 1)],
                        in0=RC[:, 4096 + TO * (2 * j_):4096 + TO * (2 * j_ + 1)],
                        in1=RC[:, 4096 + TO * (2 * j_ + 1):4096 + TO * (2 * j_ + 2)], op=ALU.add))(j),
                        wait=[(S, s_sq)] if j == 0 else [], inc=[(V, 1)])
                for j in range(2):
                    C.emit("vector", (lambda j_: lambda e: e.tensor_tensor(
                        out=RC[:, 4096 + TO * j_:4096 + TO * (j_ + 1)],
                        in0=RC[:, SL + TO * (2 * j_):SL + TO * (2 * j_ + 1)],
                        in1=RC[:, SL + TO * (2 * j_ + 1):SL + TO * (2 * j_ + 2)], op=ALU.add))(j),
                        inc=[(V, 1)])
                C.emit("vector", lambda e: e.tensor_tensor(
                    out=RC[:, SL:SL + TO], in0=RC[:, 4096:4096 + TO],
                    in1=RC[:, 4096 + TO:4096 + 2 * TO], op=ALU.add), inc=[(V, 1)])
                if g == 0:
                    C.emit("vector", lambda e: e.tensor_copy(rmsacc[:, 0:TO], RC[:, SL:SL + TO]), inc=[(V, 1)])
                else:
                    C.emit("vector", (lambda g_: lambda e: e.tensor_tensor(
                        out=rmsacc[:, TO * (g_ % 2):TO * (g_ % 2) + TO],
                        in0=rmsacc[:, TO * ((g_ - 1) % 2):TO * ((g_ - 1) % 2) + TO],
                        in1=RC[:, SL:SL + TO], op=ALU.add))(g), inc=[(V, 1)])
                for m in range(8):
                    mg = 8 * g + m
                    half, loc = (RA, mg) if mg < 16 else (RB, mg - 16)
                    C.emit("vector", (lambda m_, h_, l_, mg_: lambda e: e.tensor_scalar(
                        out=h_[:, 3072 + TO * l_:3072 + TO * (l_ + 1)], in0=RB[:, TO * m_:TO * (m_ + 1)],
                        scalar1=nw_sb[:, mg_:mg_ + 1], scalar2=None, op0=ALU.mult))(m, half, loc, mg),
                        inc=[(V, 1)])
            v_att = V.count
            C.emit("tensor", lambda e: e.matmul(ps[0][0:1, 0:TO], ones[:, 0:1], rmsacc[:, TO:2 * TO],
                                                start=True, stop=True),
                   wait=[(V, v_att)], inc=[(T, 1)])
            C.emit("vector", lambda e: e.tensor_scalar(
                out=aux[0:1, 0:TO], in0=ps[0][0:1, 0:TO], scalar1=1.0 / 4096.0, scalar2=1e-6,
                op0=ALU.mult, op1=ALU.add), wait=[(T, T.count)], inc=[(V, 1)])
            C.emit("scalar", lambda e: e.sqrt(aux[0:1, 256:256 + TO], aux[0:1, 0:TO]),
                   wait=[(V, V.count)], inc=[(S, 1)])
            C.emit("vector", lambda e: e.reciprocal(aux[0:1, 512:512 + TO], aux[0:1, 256:256 + TO]),
                   wait=[(S, S.count), (V, V.count)], inc=[(V, 1)])
            v_rr = V.count
            for i in range(2):
                C.emit("tensor", (lambda i_: lambda e: e.transpose(
                    ps[1 + i_][:, 0:1], aux[0:1, 512 + 128 * i_:512 + 128 * (i_ + 1)], ident[0:1, 0:1]))(i),
                    wait=[(V, v_rr if i == 0 else V.count)], inc=[(T, 1)])
                C.emit("vector", (lambda i_: lambda e: e.tensor_copy(
                    stats[:, 28 + i_:29 + i_], ps[1 + i_][:, 0:1]))(i),
                    wait=[(T, T.count)], inc=[(V, 1)])
            v_rrms = V.count
            tk = []
            for k in range(32):
                slot = (k % 2) * 2048
                half, loc = (RA, k) if k < 16 else (RB, k - 16)
                C.emit("sync", (lambda k_, s_: lambda e: e.dma_start(
                    out=wob[:, s_:s_ + 2048], in_=wo[128 * k_:128 * (k_ + 1), :]))(k, slot),
                    wait=[(T, tk[k - 2])] if k >= 2 else [(V, v_rrms)], inc=[(LD, 16)])
                ldo = LD.count
                for mc in range(2):
                    for n in range(4):
                        C.emit("tensor", (lambda k_, mc_, n_, s_, h_, l_: lambda e: e.matmul(
                            ps[mc_ * 4 + n_][:, 0:512],
                            h_[:, 3072 + TO * l_ + 128 * mc_:3072 + TO * l_ + 128 * (mc_ + 1)],
                            wob[:, s_ + 512 * n_:s_ + 512 * (n_ + 1)],
                            start=(k_ == 0), stop=(k_ == 31)))(k, mc, n, slot, half, loc),
                            wait=([(LD, ldo)] if (mc == 0 and n == 0) else []),
                            inc=([(T, 1)] if (mc == 1 and n == 3) else []))
                tk.append(T.count)
            t_wo = T.count
            for mc in range(2):
                for n in range(4):
                    C.emit("vector", (lambda mc_, n_: lambda e: e.tensor_scalar(
                        out=SX[:, 2048 * mc_ + 512 * n_:2048 * mc_ + 512 * (n_ + 1)],
                        in0=ps[mc_ * 4 + n_][:, 0:512], scalar1=stats[:, 28 + mc_:29 + mc_],
                        scalar2=None, op0=ALU.mult))(mc, n),
                        wait=[(T, t_wo), (V, V.count)] if (mc == 0 and n == 0) else [], inc=[(V, 1)])
            v_s = V.count
            C.emit("gpsimd", (lambda sc_: lambda e: e.dma_start(
                out=s_scr[:, 4096 * sc_:4096 * (sc_ + 1)], in_=SX[:]))(sc),
                wait=[(V, v_s), (GP, GP.count)], inc=[(GP, 16)])
            for b in range(16):
                pb = ps[b % 2]
                C.emit("tensor", (lambda b_, pb_: lambda e: e.matmul(
                    pb_[:, 0:TO], baup[:], XN[:, 384 * b_ + oofs:384 * b_ + oofs + TO],
                    start=True, stop=True))(b, pb),
                    wait=[(V, v_s)] if b == 0 else [(V, V.count)], inc=[(T, 1)])
                tb = T.count
                C.emit("vector", (lambda b_, pb_: lambda e: e.tensor_copy(
                    RA[:, TO * b_:TO * (b_ + 1)], pb_[:, 0:TO]))(b, pb),
                    wait=[(T, tb)], inc=[(V, 1)])
            v_za = V.count
            C.emit("gpsimd", (lambda h_: lambda e: e.dma_start(
                out=zscr[:, 4096 * h_:4096 * (h_ + 1)], in_=RA[:, 0:4096]))(sc % 2),
                wait=[(V, v_za), (GP, GP.count)], inc=[(GP, 16)])

        def emit_ffn_pair(pair):
            TP = 512
            xnf = XN[:]
            r_bf = RA[:].bitcast(BF16)
            C.barrier()
            gp_tile = []
            zu = [zscr[:, 4096 * h:4096 * (h + 1)].rearrange(
                "(u c) (b t) -> c b u t", u=8, c=16, b=16) for h in range(2)]
            for c in range(16):
                for h in range(2):
                    C.emit("gpsimd", (lambda c_, h_: lambda e: e.dma_start(
                        out=RB[:, TP * c_ + 256 * h_:TP * c_ + 256 * (h_ + 1)],
                        in_=zu[h_][c_]))(c, h),
                        wait=[(GP, GP.count)] if c == 0 and h == 0 else [], inc=[(GP, 16)])
                gp_tile.append(GP.count)
            gp_zb = GP.count
            for c in range(16):
                pb = ps[2 + c % 2]
                C.emit("tensor", (lambda c_, pb_: lambda e: e.matmul(
                    pb_[:, 0:TP], h128[:], RB[:, TP * c_:TP * (c_ + 1)],
                    start=True, stop=True))(c, pb),
                    wait=[(GP, gp_tile[c])] + ([(V, V.count)] if c > 0 else []), inc=[(T, 1)])
                tb = T.count
                C.emit("vector", (lambda c_, pb_: lambda e: e.tensor_copy(
                    RA[:, TP * c_:TP * (c_ + 1)], pb_[:, 0:TP]))(c, pb),
                    wait=[(T, tb)], inc=[(V, 1)])
            v_xh = V.count
            t_sb = T.count
            r1T = xn_bf[:, 0:8192]
            C.emit("scalar", lambda e: e.activation(
                out=RC[:, 0:8192], in_=RA[:, 0:8192], func=ACT.Abs),
                wait=[(V, v_xh)], inc=[(S, 1)])
            s_abs = S.count
            C.emit("vector", lambda e: e.tensor_tensor(
                out=RB[:, 0:4096], in0=RC[:, 0:4096], in1=RC[:, 4096:8192], op=ALU.max),
                wait=[(S, s_abs), (T, t_sb)], inc=[(V, 1)])
            C.emit("vector", lambda e: e.tensor_tensor(
                out=RB[:, 4096:6144], in0=RB[:, 0:2048], in1=RB[:, 2048:4096], op=ALU.max),
                inc=[(V, 1)])
            C.emit("vector", lambda e: e.tensor_tensor(
                out=RB[:, 0:1024], in0=RB[:, 4096:5120], in1=RB[:, 5120:6144], op=ALU.max),
                inc=[(V, 1)])
            C.emit("vector", lambda e: e.tensor_tensor(
                out=RB[:, 6144:6656], in0=RB[:, 0:512], in1=RB[:, 512:1024], op=ALU.max),
                inc=[(V, 1)])
            v_fold = V.count

            def mrow_chain(src_off, m_col, t):
                C.emit("tensor", (lambda t_, so_: lambda e: e.transpose(
                    ps[1][:, 0:128], RB[:, so_ + 128 * t_:so_ + 128 * (t_ + 1)], ident[:]))(t, src_off),
                    wait=[(V, V.count), (S, S.count)], inc=[(T, 1)])
                C.emit("vector", lambda e: e.tensor_reduce(
                    out=stats[:, 24:25], in_=ps[1][:, 0:128], axis=AX.X, op=ALU.max),
                    wait=[(T, T.count), (V, V.count)], inc=[(V, 1)])
                C.emit("vector", (lambda t_, mc_: lambda e: e.tensor_scalar(
                    out=stats[:, mc_ + t_:mc_ + t_ + 1], in0=stats[:, 24:25], scalar1=1e-5,
                    scalar2=None, op0=ALU.max))(t, m_col), wait=[(V, V.count)], inc=[(V, 1)])
                C.emit("vector", (lambda t_, mc_: lambda e: e.reciprocal(
                    stats[:, 25:26], stats[:, mc_ + t_:mc_ + t_ + 1]))(t, m_col),
                    wait=[(V, V.count)], inc=[(V, 1)])
                C.emit("vector", lambda e: e.tensor_scalar(
                    out=stats[:, 26:27], in0=stats[:, 25:26], scalar1=127.0, scalar2=None,
                    op0=ALU.mult), wait=[(V, V.count)], inc=[(V, 1)])
                C.emit("tensor", lambda e: e.transpose(
                    ps[3][0:1, 0:128], stats[:, 26:27], ident[:]),
                    wait=[(V, V.count)], inc=[(T, 1)])
                C.emit("vector", (lambda t_: lambda e: e.tensor_copy(
                    aux[0:1, 256 + 128 * t_:256 + 128 * (t_ + 1)], ps[3][0:1, 0:128]))(t),
                    wait=[(T, T.count)], inc=[(V, 1)])

            for t in range(4):
                mrow_chain(6144, 12, t)
            C.emit("tensor", lambda e: e.matmul(
                ps[4][:, 0:TP], ones[0:1, 0:128], aux[0:1, 256:256 + TP],
                start=True, stop=True), wait=[(V, V.count)], inc=[(T, 1)])
            t_bc = T.count
            for c in range(16):
                C.emit("vector", (lambda c_: lambda e: e.tensor_tensor(
                    out=RC[:, TP * c_:TP * (c_ + 1)],
                    in0=RA[:, TP * c_:TP * (c_ + 1)],
                    in1=ps[4][:, 0:TP], op=ALU.mult))(c),
                    wait=[(T, t_bc)] if c == 0 else [], inc=[(V, 1)])
            C.emit("vector", lambda e: e.tensor_scalar(
                out=RB[:, 0:8192], in0=RC[:, 0:8192], scalar1=384.5, scalar2=None,
                op0=ALU.add), inc=[(V, 1)])
            C.emit("vector", lambda e: e.tensor_single_scalar(
                out=RC[:, 0:8192].bitcast(I32), in_=RB[:, 0:8192].bitcast(I32),
                scalar=-32768, op=ALU.bitwise_and), inc=[(V, 1)])
            C.emit("vector", lambda e: e.tensor_scalar(
                out=r1T, in0=RC[:, 0:8192], scalar1=-384.0, scalar2=None,
                op0=ALU.add), inc=[(V, 1)])
            v_r1 = V.count
            tku = []
            vr_og = []
            sq_og = []
            for og in range(4):
                for k in range(16):
                    slot = (k % 2) * 1024
                    src = wupT_scr.rearrange("(q c) o -> c q o", c=16)[k, :, 1024 * og:1024 * (og + 1)]
                    ki = og * 16 + k
                    C.emit("sync", (lambda s_, src_: lambda e: e.dma_start(
                        out=wub[:, s_:s_ + 1024], in_=src_))(slot, src),
                        wait=[(T, tku[ki - 2])] if ki >= 2 else [(V, v_r1), (GP, gp_wdone)], inc=[(LD, 16)])
                    ldu = LD.count
                    for m in range(8):
                        kw = []
                        if m == 0:
                            kw = [(LD, ldu)]
                            if k == 0:
                                kw += [(V, v_r1)] if og == 0 else [(V, vr_og[og - 1])]
                        C.emit("tensor", (lambda k_, m_, s_: lambda e: e.matmul(
                            ps[m_][:, 0:TP], wub[:, s_ + 128 * m_:s_ + 128 * (m_ + 1)],
                            r1T[:, TP * k_:TP * (k_ + 1)],
                            start=(k_ == 0), stop=(k_ == 15)))(k, m, slot),
                            wait=kw,
                            inc=([(T, 1)] if m == 7 else []))
                    tku.append(T.count)
                t_up = T.count
                half = RA if og < 2 else RB
                base = 4096 * (og % 2)
                rcb = 4096 * (og % 2)
                for m in range(8):
                    rw = [(T, t_up)] if m == 0 else []
                    if m == 0 and og >= 2:
                        rw += [(S, sq_og[og - 2])]
                    C.emit("vector", (lambda m_, rb_: lambda e: e.tensor_scalar(
                        out=RC[:, rb_ + TP * m_:rb_ + TP * (m_ + 1)], in0=ps[m_][:, 0:TP],
                        scalar1=0.0, scalar2=None, op0=ALU.max))(m, rcb),
                        wait=rw, inc=[(V, 1)])
                vr_og.append(V.count)
                for m in range(8):
                    C.emit("scalar", (lambda m_, h_, b_, rb_: lambda e: e.square(
                        h_[:, b_ + TP * m_:b_ + TP * (m_ + 1)],
                        RC[:, rb_ + TP * m_:rb_ + TP * (m_ + 1)]))(m, half, base, rcb),
                        wait=[(V, vr_og[-1])] if m == 0 else [], inc=[(S, 1)])
                sq_og.append(S.count)
            s_h1 = S.count
            for hb in range(2):
                for b in range(16):
                    idx = hb * 16 + b
                    half = RA if idx < 16 else RB
                    base = TP * (idx % 16)
                    pb = ps[b % 2]
                    C.emit("tensor", (lambda h_, ba_, pb_: lambda e: e.matmul(
                        pb_[:, 0:TP], badn[:], h_[:, ba_:ba_ + TP],
                        start=True, stop=True))(half, base, pb),
                        wait=[(GP, GP.count), (S, s_h1), (V, V.count)] if b == 0 else [(V, V.count)],
                        inc=[(T, 1)])
                    tb = T.count
                    C.emit("vector", (lambda b_, pb_: lambda e: e.tensor_copy(
                        RC[:, TP * b_:TP * (b_ + 1)], pb_[:, 0:TP]))(b, pb),
                        wait=[(T, tb)], inc=[(V, 1)])
                v_za2 = V.count
                C.emit("gpsimd", (lambda h_: lambda e: e.dma_start(
                    out=zscr[:, 8192 * h_:8192 * (h_ + 1)], in_=RC[:, 0:8192]))(hb),
                    wait=[(V, v_za2), (GP, GP.count)], inc=[(GP, 16)])
            zt = zscr[:, 0:16384].rearrange("(u c) (b t) -> c b u t", u=4, c=32, b=32)
            gp_tile2 = []
            for c in range(32):
                half = RA if c < 16 else RB
                C.emit("gpsimd", (lambda c_, h_: lambda e: e.dma_start(
                    out=h_[:, 512 * (c_ % 16):512 * (c_ % 16 + 1)], in_=zt[c_]))(c, half),
                    wait=[(GP, GP.count)] if c == 0 else [], inc=[(GP, 16)])
                gp_tile2.append(GP.count)
            gp_zb2 = GP.count
            def xh2_dst(c):
                if c < 16:
                    return RC[:, TP * c:TP * (c + 1)]
                if c < 24:
                    return SX[:, TP * (c - 16):TP * (c - 15)]
                return xnf[:, TP * (c - 24):TP * (c - 23)]
            for c in range(32):
                pb = ps[2 + c % 2]
                src_half = RA if c < 16 else RB
                C.emit("tensor", (lambda c_, pb_, sh_: lambda e: e.matmul(
                    pb_[:, 0:TP], h128[:], sh_[:, TP * (c_ % 16):TP * (c_ % 16 + 1)],
                    start=True, stop=True))(c, pb, src_half),
                    wait=[(GP, gp_tile2[c])] + ([(V, V.count)] if c > 0 else []), inc=[(T, 1)])
                tb = T.count
                C.emit("vector", (lambda c_, pb_: lambda e: e.tensor_copy(
                    xh2_dst(c_), pb_[:, 0:TP]))(c, pb),
                    wait=[(T, tb)], inc=[(V, 1)])
            v_xh2 = V.count
            t_sb2 = T.count
            H0, H1, H2, H3 = 6144, 6656, 7168, 7680

            def tree512(src, width, op, dst_off):
                w = width // 2
                C.emit("vector", (lambda w_, s_: lambda e: e.tensor_tensor(
                    out=RB[:, 0:w_] if w_ > 512 else RB[:, dst_off:dst_off + 512],
                    in0=s_[:, 0:w_], in1=s_[:, w_:2 * w_], op=op))(w, src), inc=[(V, 1)])
                cur, nxt = 0, 4096
                while w > 512:
                    w //= 2
                    dsto = nxt if w > 512 else dst_off
                    C.emit("vector", (lambda c_, d_, w_: lambda e: e.tensor_tensor(
                        out=RB[:, d_:d_ + w_], in0=RB[:, c_:c_ + w_], in1=RB[:, c_ + w_:c_ + 2 * w_],
                        op=op))(cur, dsto, w), inc=[(V, 1)])
                    cur, nxt = nxt, cur

            def region_absmax(src, width, dst_off, tmp1, tmp2):
                tree512(src, width, ALU.max, tmp1)
                tree512(src, width, ALU.min, dst_off)
                C.emit("vector", (lambda d_, t_: lambda e: e.tensor_scalar(
                    out=RB[:, t_:t_ + 512], in0=RB[:, d_:d_ + 512], scalar1=-1.0, scalar2=None,
                    op0=ALU.mult))(dst_off, tmp2), inc=[(V, 1)])
                C.emit("vector", (lambda d_, t1_, t2_: lambda e: e.tensor_tensor(
                    out=RB[:, d_:d_ + 512], in0=RB[:, t1_:t1_ + 512], in1=RB[:, t2_:t2_ + 512],
                    op=ALU.max))(dst_off, tmp1, tmp2), inc=[(V, 1)])

            C.emit("vector", None, wait=[(V, v_xh2), (T, t_sb2)])
            region_absmax(RC[:, 0:8192], 8192, H0, H1, H2)
            region_absmax(SX[:, 0:4096], 4096, H1, H2, H3)
            C.emit("vector", lambda e: e.tensor_tensor(
                out=RB[:, H2:H2 + 512], in0=RB[:, H0:H0 + 512], in1=RB[:, H1:H1 + 512],
                op=ALU.max), inc=[(V, 1)])
            region_absmax(xnf[:, 0:4096], 4096, H0, H1, H3)
            C.emit("vector", lambda e: e.tensor_tensor(
                out=RB[:, H3:H3 + 512], in0=RB[:, H2:H2 + 512], in1=RB[:, H0:H0 + 512],
                op=ALU.max), inc=[(V, 1)])
            for t in range(4):
                mrow_chain(H3, 16, t)
            C.emit("tensor", lambda e: e.matmul(
                ps[4][:, 0:TP], ones[0:1, 0:128], aux[0:1, 256:256 + TP],
                start=True, stop=True), wait=[(V, V.count)], inc=[(T, 1)])
            t_bc2 = T.count
            C.emit("vector", lambda e: e.tensor_copy(aux[:, 0:TP], ps[4][:, 0:TP]),
                   wait=[(T, t_bc2)], inc=[(V, 1)])
            r2T = r_bf[:, 0:16384]

            def scale_region(src, n_tiles, bf_off, first):
                for c in range(n_tiles):
                    C.emit("vector", (lambda c_, s_: lambda e: e.tensor_tensor(
                        out=RB[:, TP * c_:TP * (c_ + 1)],
                        in0=s_[:, TP * c_:TP * (c_ + 1)],
                        in1=aux[:, 0:TP], op=ALU.mult))(c, src),
                        inc=[(V, 1)])
                w = TP * n_tiles
                C.emit("vector", (lambda w_: lambda e: e.tensor_scalar(
                    out=RC[:, 0:w_], in0=RB[:, 0:w_], scalar1=384.5, scalar2=None,
                    op0=ALU.add))(w), inc=[(V, 1)])
                C.emit("vector", (lambda w_: lambda e: e.tensor_single_scalar(
                    out=RB[:, 0:w_].bitcast(I32), in_=RC[:, 0:w_].bitcast(I32),
                    scalar=-32768, op=ALU.bitwise_and))(w), inc=[(V, 1)])
                C.emit("vector", (lambda w_, bo_: lambda e: e.tensor_scalar(
                    out=r2T[:, bo_:bo_ + w_], in0=RB[:, 0:w_], scalar1=-384.0, scalar2=None,
                    op0=ALU.add))(w, bf_off), inc=[(V, 1)])

            scale_region(RC[:, 0:8192], 16, 0, True)
            v_sA = V.count
            scale_region(SX[:, 0:4096], 8, 8192, False)
            v_sB = V.count
            scale_region(xnf[:, 0:4096], 8, 12288, False)
            v_sC = V.count
            v_r2 = V.count
            for t in range(4):
                C.emit("vector", (lambda t_: lambda e: e.tensor_scalar(
                    out=stats[:, 24:25], in0=stats[:, 12 + t_:13 + t_], scalar1=stats[:, 8:9],
                    scalar2=1.0 / 127.0, op0=ALU.mult, op1=ALU.mult))(t),
                    wait=[(V, V.count)], inc=[(V, 1)])
                C.emit("scalar", lambda e: e.square(stats[:, 26:27], stats[:, 24:25]),
                       wait=[(V, V.count)], inc=[(S, 1)])
                C.emit("vector", (lambda t_: lambda e: e.tensor_scalar(
                    out=stats[:, 25:26], in0=stats[:, 16 + t_:17 + t_], scalar1=stats[:, 10:11],
                    scalar2=1.0 / 127.0, op0=ALU.mult, op1=ALU.mult))(t),
                    wait=[(S, S.count), (V, V.count)], inc=[(V, 1)])
                C.emit("vector", (lambda t_: lambda e: e.tensor_tensor(
                    out=stats[:, 20 + t_:21 + t_], in0=stats[:, 26:27], in1=stats[:, 25:26],
                    op=ALU.mult))(t), wait=[(V, V.count)], inc=[(V, 1)])
            v_f = V.count
            for p2 in range(2):
                tkd = []
                for k in range(32):
                    slot = (k % 2) * 1024
                    src = wdnT_scr.rearrange("(q c) o -> c q o", c=32)[k, :, 1024 * p2:1024 * (p2 + 1)]
                    C.emit("sync", (lambda s_, src_: lambda e: e.dma_start(
                        out=wdb[:, s_:s_ + 1024], in_=src_))(slot, src),
                        wait=[(T, tkd[k - 2])] if k >= 2 else ([(V, v_sA), (GP, gp_wdone)] if p2 == 0 else [(V, V.count)]),
                        inc=[(LD, 16)])
                    ldd = LD.count
                    for mc in range(4):
                        for n in range(2):
                            gate = []
                            if mc == 0 and n == 0:
                                gate = [(LD, ldd)]
                                if k == 0:
                                    gate += [(V, v_sA)] if p2 == 0 else [(V, V.count)]
                                elif k == 16 and p2 == 0:
                                    gate += [(V, v_sB)]
                                elif k == 24 and p2 == 0:
                                    gate += [(V, v_sC)]
                            C.emit("tensor", (lambda k_, mc_, n_, s_: lambda e: e.matmul(
                                ps[mc_ * 2 + n_][:, 0:TP], r2T[:, TP * k_ + 128 * mc_:TP * k_ + 128 * (mc_ + 1)],
                                wdb[:, s_ + 512 * n_:s_ + 512 * (n_ + 1)],
                                start=(k_ == 0), stop=(k_ == 31)))(k, mc, n, slot),
                                wait=gate,
                                inc=([(T, 1)] if (mc == 3 and n == 1) else []))
                    tkd.append(T.count)
                t_dn = T.count
                for mc in range(4):
                    for n in range(2):
                        C.emit("vector", (lambda mc_, n_, p_: lambda e: e.tensor_scalar(
                            out=RC[:, 2048 * mc_ + 1024 * p_ + 512 * n_:2048 * mc_ + 1024 * p_ + 512 * (n_ + 1)],
                            in0=ps[mc_ * 2 + n_][:, 0:TP], scalar1=stats[:, 20 + mc_:21 + mc_],
                            scalar2=None, op0=ALU.mult))(mc, n, p2),
                            wait=[(T, t_dn), (V, V.count)] if (mc == 0 and n == 0) else [], inc=[(V, 1)])
            t_dn_all = T.count
            v_dn = V.count
            for mc in range(4):
                dst = SX[:, 2048 * mc:2048 * (mc + 1)] if mc < 2 else xnf[:, 2048 * (mc - 2):2048 * (mc - 1)]
                C.emit("scalar", (lambda mc_, d_: lambda e: e.activation(
                    out=d_, in_=RC[:, 2048 * mc_:2048 * (mc_ + 1)], func=ACT.Silu))(mc, dst),
                    wait=[(V, v_dn)] if mc == 0 else [], inc=[(S, 1)])
            s_g = S.count
            for i in range(4):
                xrow = HALO + TP * pair + 128 * i
                C.emit("sync", (lambda i_, r0: lambda e: e.dma_start(
                    out=RB[:, 2048 * i_:2048 * (i_ + 1)], in_=x_in[r0:r0 + 128, :]))(i, xrow),
                    wait=[(T, t_dn_all)] if i == 0 else [], inc=[(LD, 16)])
            ld_x2 = LD.count
            C.emit("gpsimd", (lambda p_: lambda e: e.dma_start(
                out=RA[:, 0:8192], in_=s_scr[:, 8192 * p_:8192 * (p_ + 1)]))(pair),
                wait=[(T, t_dn_all), (GP, GP.count)], inc=[(GP, 16)])
            gp_s = GP.count
            for i in range(4):
                C.emit("vector", (lambda i_: lambda e: e.tensor_tensor(
                    out=RC[:, 2048 * i_:2048 * (i_ + 1)],
                    in0=SX[:, 2048 * i_:2048 * (i_ + 1)] if i_ < 2 else xnf[:, 2048 * (i_ - 2):2048 * (i_ - 1)],
                    in1=RA[:, 2048 * i_:2048 * (i_ + 1)], op=ALU.mult))(i),
                    wait=[(S, s_g), (LD, ld_x2), (GP, gp_s)] if i == 0 else [], inc=[(V, 1)])
                C.emit("vector", (lambda i_: lambda e: e.tensor_tensor(
                    out=RA[:, 2048 * i_:2048 * (i_ + 1)], in0=RB[:, 2048 * i_:2048 * (i_ + 1)],
                    in1=RC[:, 2048 * i_:2048 * (i_ + 1)], op=ALU.add))(i),
                    inc=[(V, 1)])
                vres = V.count
                row0 = TP * pair + 128 * i
                C.emit("gpsimd", (lambda i_, r0: lambda e: e.dma_start(
                    out=out_hbm[r0:r0 + 128, :], in_=RA[:, 2048 * i_:2048 * (i_ + 1)]))(i, row0),
                    wait=[(V, vres)], inc=[(GP, 16)])

        for _pair in range(2):
            emit_attn_half(2 * _pair)
            emit_attn_half(2 * _pair + 1)
            emit_ffn_pair(_pair)

        C.emit("gpsimd", None, wait=[(GP, GP.count)])

        with nc.Block() as block:
            @block.sync
            def _(e):
                C.replay("sync", e)

            @block.gpsimd
            def _(e):
                C.replay("gpsimd", e)

            @block.tensor
            def _(e):
                C.replay("tensor", e)

            @block.vector
            def _(e):
                C.replay("vector", e)

            @block.scalar
            def _(e):
                C.replay("scalar", e)

    return nc


_NC_CACHE = None


def kernel(L0_attn_x, L0_attn_Wq1, L0_attn_Wq2, L0_attn_Wk1, L0_attn_Wk2,
           L0_attn_Wv, L0_attn_Wo, L0_attn_norm_w, L0_ffn_w_up, L0_ffn_w_down):
    global _NC_CACHE
    x = np.ascontiguousarray(np.asarray(L0_attn_x)[0], dtype=np.float32)
    H16, H32, H128 = _hadamard(16), _hadamard(32), _hadamard(128)
    ba_up = (np.kron(np.eye(8, dtype=np.float32), H16) * np.float32(1.0 / np.sqrt(2048.0))).astype(np.float32)
    ba_dn = (np.kron(np.eye(4, dtype=np.float32), H32) * np.float32(2.0 ** -6)).astype(np.float32)
    ident = np.eye(128, dtype=np.float32)
    common = {
        "wq1": np.ascontiguousarray(L0_attn_Wq1, np.float32),
        "wq2": np.ascontiguousarray(L0_attn_Wq2, np.float32),
        "wk1": np.ascontiguousarray(L0_attn_Wk1, np.float32),
        "wk2": np.ascontiguousarray(L0_attn_Wk2, np.float32),
        "wv": np.ascontiguousarray(L0_attn_Wv, np.float32),
        "wo": np.ascontiguousarray(L0_attn_Wo, np.float32),
        "norm_w": np.ascontiguousarray(np.asarray(L0_attn_norm_w, np.float32).reshape(32, 128).T),
        "w_up": np.ascontiguousarray(L0_ffn_w_up, np.float32),
        "w_down": np.ascontiguousarray(L0_ffn_w_down, np.float32),
        "ident": ident, "ba_up": ba_up, "ba_dn": ba_dn, "h128": H128,
    }
    in_maps = []
    for c in range(8):
        lo = S_CORE * c
        halo = np.zeros((HALO, E), np.float32) if c == 0 else x[lo - HALO:lo]
        x_pad = np.ascontiguousarray(np.concatenate([halo, x[lo:lo + S_CORE]], 0))
        in_maps.append({"x_pad": x_pad, **common})
    if _NC_CACHE is None:
        _NC_CACHE = build_nc()
    res = run_bass_kernel_spmd(_NC_CACHE, in_maps, core_ids=list(range(8)))
    out = np.concatenate([res.results[c]["out"] for c in range(8)], 0)
    return out[None].astype(np.float32)
